# version 17
# baseline (speedup 1.0000x reference)
import contextlib
import numpy as np
import concourse.bass as bass
import concourse.mybir as mybir
from concourse.bass_utils import run_bass_kernel_spmd

F32 = mybir.dt.float32
BF16 = mybir.dt.bfloat16
AF = mybir.ActivationFunctionType
ALU = mybir.AluOpType
AX = mybir.AxisListType

NCORES = 8
TOK = 2048
DM = 2048
CC, KH = 32, 64
HID = 8192
S = 512
NSUP = TOK // S
SUB = 128
NSUB = S // SUB
DG = 4
NDG = KH // DG
MACRO = 2048
NMAC = HID // MACRO
NSTEP = MACRO // 128
EPS = 1e-5
POOL_GRID_FRAC = 2


def build_program(debug=False):
    nc = bass.Bass("TRN2", target_bir_lowering=False)

    x_in = nc.dram_tensor("x", [TOK, DM], F32, kind="ExternalInput").ap()
    wqkv = nc.dram_tensor("wqkv", [CC, KH, 3 * KH], F32, kind="ExternalInput").ap()
    bqkv = nc.dram_tensor("bqkv", [CC, 3 * KH], F32, kind="ExternalInput").ap()
    wout = nc.dram_tensor("wout", [KH, KH], F32, kind="ExternalInput").ap()
    bout = nc.dram_tensor("bout", [KH], F32, kind="ExternalInput").ap()
    wg = nc.dram_tensor("wg", [HID, DM], F32, kind="ExternalInput").ap()
    wu = nc.dram_tensor("wu", [HID, DM], F32, kind="ExternalInput").ap()
    wd = nc.dram_tensor("wd", [DM, HID], F32, kind="ExternalInput").ap()
    out_t = nc.dram_tensor("out", [TOK, DM], F32, kind="ExternalOutput").ap()

    dbg = {}
    if debug:
        def dbg_t(name, shape, dt):
            dbg[name] = nc.dram_tensor("dbg_" + name, shape, dt, kind="ExternalOutput").ap()
        dbg_t("qkv", [128, CC * 192], BF16)
        dbg_t("ctx", [128, DM], BF16)
        dbg_t("ctxf", [128, DM], BF16)
        dbg_t("yb", [128, DM], F32)
        dbg_t("xmid_last", [128, DM], F32)
        dbg_t("ln1f", [128, DM], BF16)
        dbg_t("lnsc", [128, DM], BF16)
        dbg_t("Zt", [128, 32 * DG], F32)
        dbg_t("Wt", [128, 32 * DG], BF16)
        dbg_t("P0", [128, 2048], BF16)
        dbg_t("P1", [128, 2048], BF16)
        dbg_t("acc", [128, NSUB * DM], F32)
        dbg_t("ln3f", [128, 16 * S], BF16)
        dbg_t("stt", [128, 32], F32)
        dbg_t("gated", [128, NSTEP * S], BF16)
        dbg_t("xmid_all", [TOK, DM], F32)
        dbg_t("wgT", [128, 2048], BF16)

    wg_bf = nc.dram_tensor("wg_bf", [HID, DM], BF16, kind="Internal").ap()
    wu_bf = nc.dram_tensor("wu_bf", [HID, DM], BF16, kind="Internal").ap()
    wd_bf = nc.dram_tensor("wd_bf", [DM, HID], BF16, kind="Internal").ap()
    scr_gT = nc.dram_tensor("scr_gT", [DM, HID], BF16, kind="Internal").ap()
    scr_uT = nc.dram_tensor("scr_uT", [DM, HID], BF16, kind="Internal").ap()
    scr_dT = nc.dram_tensor("scr_dT", [HID, DM], BF16, kind="Internal").ap()
    xmid_d = nc.dram_tensor("xmid", [TOK, DM], F32, kind="Internal").ap()

    stk = contextlib.ExitStack()
    with stk:
        def sb(name, shape, dt):
            return stk.enter_context(nc.sbuf_tensor(name, shape, dt))

        acc = sb("acc", [128, NSUB * DM], F32)
        ln3f = [sb("ln3a", [128, 16 * S], BF16), sb("ln3b", [128, 16 * S], BF16)]
        wgT = [sb("wgT0", [128, 2048], BF16), sb("wgT1", [128, 2048], BF16)]
        wuT = [sb("wuT0", [128, 2048], BF16), sb("wuT1", [128, 2048], BF16)]
        wdT = [sb("wdT0", [128, 512], BF16), sb("wdT1", [128, 512], BF16)]
        gated = sb("gated", [128, NSTEP * S], BF16)
        silu_sb = sb("silu", [128, 512], BF16)
        gate_sb = sb("gate_sb", [128, 512], BF16)
        qkv = sb("qkv", [128, CC * 192], BF16)
        P0 = sb("P0", [128, 1024 * DG], BF16)
        P1 = [sb("P1a", [128, 1024 * DG], BF16), sb("P1b", [128, 1024 * DG], BF16)]
        Zt = sb("Zt", [128, 32 * DG], F32)
        Wt = sb("Wt", [128, 32 * DG], BF16)
        ctx = sb("ctx", [128, DM], BF16)
        ctxf = sb("ctxf", [128, DM], BF16)
        yb = sb("yb", [128, DM], BF16)
        lnsc = sb("lnsc", [128, DM], BF16)
        ln1f = sb("ln1f", [128, DM], BF16)
        xbuf = [sb("xa", [128, DM], F32), sb("xb", [128, DM], F32)]
        xm = sb("xm", [128, DM], F32)
        wq_s = sb("wq", [128, 16 * 192], BF16)
        wo_s = sb("wo", [128, 64], BF16)
        bq_s = sb("bq", [1, CC * 192], BF16)
        bo_s = sb("bo", [1, 64], BF16)
        ones = sb("ones", [1, 128], BF16)
        ident = sb("ident", [128, 128], BF16)
        stt = sb("stats", [128, 32], F32)
        epsT = sb("epsT", [128, 1], F32)
        mnegT = sb("mnegT", [128, 1], F32)
        zsc = sb("zsc", [128, 2048], BF16)

        import os as _os
        if _os.environ.get("SBUF_DEBUG"):
            print("SBUF remaining after allocs:", nc.sbuf_bytes_remaining)
        psum = [stk.enter_context(nc.psum_tensor(f"ps{i}", [128, 512], F32))
                for i in range(8)]

        class Sem:
            __slots__ = ("h", "n")

            def __init__(self, name):
                self.h = stk.enter_context(nc.semaphore(name))
                self.n = 0

        s_x = Sem("s_x")
        s_xmst = Sem("s_xmst")
        s_out = Sem("s_out")
        s_attw = Sem("s_attw")
        s_conv = Sem("s_conv")
        s_w = [Sem("s_w0"), Sem("s_w1")]
        s_cT = Sem("s_cT")
        s_silu = Sem("s_silu")
        s_gated = Sem("s_gated")
        s_dnev = Sem("s_dnev")
        s_wd = [Sem("s_wd0"), Sem("s_wd1")]
        sPE = Sem("sPE")
        sDVE = Sem("sDVE")
        sACT = Sem("sACT")
        sPOOL = Sem("sPOOL")
        ESEM = {"tensor": sPE, "vector": sDVE, "scalar": sACT, "gpsimd": sPOOL}

        sched = {k: [] for k in ("sync", "gpsimd", "tensor", "vector", "scalar")}

        def em(eng, fn):
            sched[eng].append(fn)

        def wait(eng, sem, val=None):
            v = sem.n if val is None else val
            if v > 0:
                em(eng, lambda e, h=sem.h, vv=v: e.wait_ge(h, vv))

        def op(eng, fn):
            se = ESEM[eng]
            se.n += 1
            em(eng, lambda e, f=fn, h=se.h: f(e).then_inc(h, 1))
            return se.n

        def op_det(eng, fn, sem):
            sem.n += 1
            em(eng, lambda e, f=fn, h=sem.h: f(e).then_inc(h, 1))
            return sem.n

        def dma(eng, dst, src, sem, transpose=False):
            sem.n += 16
            em(eng, lambda e, d=dst, s=src, h=sem.h, t=transpose:
               e.dma_start(d, s, transpose=t).then_inc(h, 16))
            return sem.n

        op("gpsimd", lambda g: g.memset(ident[:, :], 0.0))
        op("gpsimd", lambda g: g.affine_select(
            out=ident[:, :], in_=ident[:, :], compare_op=ALU.not_equal,
            fill=1.0, base=0, pattern=[[-1, 128]], channel_multiplier=1))
        op("gpsimd", lambda g: g.memset(ones[:, :], 1.0))
        op("gpsimd", lambda g: g.memset(epsT[:, :], EPS))
        pool_consts = op("gpsimd", lambda g: g.memset(mnegT[:, :], -2.0))

        wq_re = (wqkv.rearrange("(cp ch) k i -> cp ch k i", ch=2)
                 .transpose([1, 2, 0, 3])
                 .rearrange("ch k cp i -> (ch k) cp i"))
        wq_dst = wq_s[:, :].rearrange("p (cp i) -> p cp i", i=192)
        for j in range(4):
            dma("gpsimd", wq_dst[:, j * 4:(j + 1) * 4, :],
                wq_re[:, j * 4:(j + 1) * 4, :], s_attw)
        dma("gpsimd", wo_s[0:64, :], wout, s_attw)
        dma("gpsimd", wo_s[64:128, :], wout, s_attw)
        dma("gpsimd", bq_s[0:1, :], bqkv.rearrange("c i -> (c i)").unsqueeze(0), s_attw)
        dma("gpsimd", bo_s[0:1, :], bout.unsqueeze(0), s_attw)
        attw_done = s_attw.n
        prologue_hooks = []

        conv_groups = []
        for grp in range(4):
            rows = slice(grp * 2048, (grp + 1) * 2048)
            grp_dmas = [(wg[rows, :], wg_bf[rows, :], False),
                        (wu[rows, :], wu_bf[rows, :], False)]
            for q in range(4):
                r2 = slice(q * 512, (q + 1) * 512)
                grp_dmas.append((wd[r2, grp * 2048:(grp + 1) * 2048],
                                 wd_bf[r2, grp * 2048:(grp + 1) * 2048], False))
            conv_groups.append(grp_dmas)
        conv_emitted = [0]

        def emit_conv_group(grp):
            if conv_emitted[0] > grp:
                return
            conv_emitted[0] = grp + 1
            for s_ap, d_ap, _ in conv_groups[grp]:
                dma("gpsimd", d_ap, s_ap, s_conv)

        def conv_cast_target(grp):
            return 16 * 6 * (grp + 1)

        convT_units = []
        for grp in range(4):
            for j in range(16):
                convT_units.append((wg_bf[grp * 2048:(grp + 1) * 2048, j * 128:(j + 1) * 128],
                                    scr_gT[j * 128:(j + 1) * 128, grp * 2048:(grp + 1) * 2048], grp))
            for j in range(16):
                convT_units.append((wu_bf[grp * 2048:(grp + 1) * 2048, j * 128:(j + 1) * 128],
                                    scr_uT[j * 128:(j + 1) * 128, grp * 2048:(grp + 1) * 2048], grp))
            for hc in range(16):
                h0 = grp * 2048 + hc * 128
                convT_units.append((wd_bf[0:2048, h0:h0 + 128],
                                    scr_dT[h0:h0 + 128, :], grp))
        convT_next = [0]
        convT_store_done = {}
        convT_group_mark = {}

        def emit_convT(nunits, convT_bufs):
            for _ in range(nunits):
                u = convT_next[0]
                if u >= len(convT_units):
                    return
                srcb, dstb, grp = convT_units[u]
                buf = convT_bufs[u % 2]
                wait("sync", s_conv, conv_cast_target(grp))
                if u >= 2:
                    wait("sync", s_cT, convT_store_done[u - 2])
                t = dma("sync", buf, srcb, s_cT, transpose=True)
                wait("sync", s_cT, t)
                convT_store_done[u] = dma("sync", dstb, buf, s_cT)
                convT_group_mark[grp] = convT_store_done[u]
                convT_next[0] = u + 1

        def conv_target(grp):
            return convT_group_mark[grp]

        inv_n = 1.0 / DM
        inv_n2 = inv_n * inv_n

        def emit_ln(xap, set_i, sq_out, apply_kind, apply_out=None):
            base = set_i * 8
            c = lambda i: stt[:, base + i:base + i + 1]
            su_done = op("vector", lambda v, o=c(0), x=xap: v.reduce_sum(o, x, AX.X))
            sq = op("scalar", lambda s, o=sq_out, x=xap, a=c(1):
                    s.activation(o, x, AF.Square, accum_out=a))
            wait("vector", sACT, sq)
            wait("vector", sDVE, su_done)
            m2 = op("vector", lambda v, o=c(2), i0=c(0):
                    v.tensor_scalar(o, i0, i0, inv_n2, ALU.mult, ALU.mult))
            wait("vector", sDVE, m2)
            ve = op("vector", lambda v, o=c(3), i0=c(1), i1=c(2):
                    v.scalar_tensor_tensor(o, i0, inv_n, i1, ALU.mult, ALU.subtract))
            wait("scalar", sDVE, ve)
            op("scalar", lambda s, o=c(6), i=c(3): s.activation(o, i, AF.Ln, bias=epsT[:, :]))
            r_done = op("scalar", lambda s, o=c(4), i=c(6):
                        s.activation(o, i, AF.Exp, scale=-0.5))
            wait("vector", sACT, r_done)
            b_done = op("vector", lambda v, o=c(5), i0=c(0), r=c(4):
                        v.tensor_scalar(o, i0, r, -inv_n, ALU.mult, ALU.mult))
            if apply_kind == "act":
                wait("scalar", sDVE, b_done)
                wait("scalar", sACT, r_done)
                done = op("scalar", lambda s, o=apply_out, x=xap, r=c(4), b=c(5):
                          s.activation(o, x, AF.Identity, bias=b, scale=r))
            else:
                wait("vector", sDVE, b_done)
                done = op("vector", lambda v, x=xap, r=c(4), b=c(5):
                          v.tensor_scalar(x, x, r, b, ALU.mult, ALU.add))
            return done

        bank_evac = [(sACT, 0), (sACT, 0)]

        def emit_transpose16(src_sb, dst_fn, pre_vec_wait=None):
            last_pe = 0
            for b in range(2):
                bank = psum[6 + b]
                pbf = bank[:, :].bitcast(BF16)
                wait("tensor", bank_evac[b][0], bank_evac[b][1])
                wait("tensor", sPOOL, pool_consts)
                for jj in range(8):
                    j = b * 8 + jj
                    last_pe = op("tensor", lambda t, o=pbf[:, jj * 128:(jj + 1) * 128],
                                 i=src_sb[:, j * 128:(j + 1) * 128]:
                                 t.transpose(o, i, ident[:, :]))
                wait("scalar", sPE, last_pe)
                if pre_vec_wait is not None and b == 0:
                    wait("scalar", pre_vec_wait[0], pre_vec_wait[1])
                bank_evac[b] = (sACT, dst_fn(b, pbf))
            return last_pe, bank_evac[1][1]

        x_load_done = [0] * (TOK // SUB)
        xmid_store_done = [0] * (TOK // SUB)
        ln3_ready = [0, 0]
        ln3_readers = [0, 0]
        wgt_free = [0, 0]
        wdt_free = [0, 0]
        qkv_grid_readers = {"vector": 0, "gpsimd": 0}
        lnsc_readers = 0
        ln1f_readers = 0
        ctxf_readers = 0
        xb_last_act = [0, 0]
        yb_free = 0
        ctx_free = 0
        gated_free = 0
        grid_exp_prev = 0
        grid_ctx_prev = 0
        grid_ctx_par = [0, 0]
        xm_free = 0

        wait("scalar", sPOOL, pool_consts)
        convT_total_mark = [0]
        def emit_attn_subtile(g, pe_gu=None, pe_down=None, ticks=None):
            ticks = ticks if ticks is not None else []

            def drain_tick():
                if ticks:
                    ticks.pop(0)()
            nonlocal lnsc_readers, ln1f_readers, ctxf_readers, yb_free, ctx_free
            nonlocal grid_exp_prev, grid_ctx_prev, qkv_grid_readers, xm_free
            sup = g // NSUB
            par = sup % 2
            ss = g % NSUB
            if g == NSUB:
                wait("scalar", s_cT, convT_total_mark[0])
            R = g * SUB
            xb = xbuf[g % 2]
            E_S = "vector"
            E_M = "vector"
            if g == 2:
                xb = xm

            if g >= 3:
                wait("sync", s_xmst, xmid_store_done[g - 2])
                wait("sync", sACT, xb_last_act[g % 2])
                x_load_done[g] = dma("sync", xb[:, :], x_in[R:R + SUB, :], s_x)
            if g < NSUB:
                convT_bufs = [ln3f[1][:, 0:2048], ln3f[1][:, 2048:4096]]
                emit_convT(48, convT_bufs)

            wait("vector", s_x, x_load_done[g])
            wait("scalar", s_x, x_load_done[g])
            wait("scalar", sPE, lnsc_readers)
            ln1_done = emit_ln(xb[:, :], 0, lnsc[:, :], "act", lnsc[:, :])

            wait("tensor", sACT, ln1_done)

            def _w_ln1(b, pbf):
                return op("scalar", lambda s, o=ln1f[:, b * 1024:(b + 1) * 1024],
                          i=pbf: s.copy(o, i))
            t1_pe, t1_evac = emit_transpose16(lnsc, _w_ln1,
                                              pre_vec_wait=(sPE, ln1f_readers))
            lnsc_readers = t1_pe

            wait("tensor", s_attw, attw_done)
            wait("tensor", sACT, t1_evac)
            qkv_evac = {}
            for cc in range(CC):
                bank = psum[6 + (cc % 2)]
                if cc >= 2:
                    wait("tensor", sACT if cc % 2 == 0 else sDVE, qkv_evac[cc - 2])
                ph, pc = cc % 2, cc // 2
                op("tensor", lambda t, o=bank[:, 0:192],
                   l=ln1f[ph * 64:ph * 64 + 64, pc * 128:pc * 128 + 128],
                   r=wq_s[ph * 64:ph * 64 + 64, pc * 192:pc * 192 + 192]:
                   t.matmul(o, l, r, start=True, stop=False))
                mm = op("tensor", lambda t, o=bank[:, 0:192],
                        r=bq_s[0:1, cc * 192:cc * 192 + 192]:
                        t.matmul(o, ones[0:1, :], r, start=False, stop=True))
                eng = "scalar" if cc % 2 == 0 else "vector"
                wait(eng, sPE, mm)
                if cc < 2:
                    wait(eng, sDVE, qkv_grid_readers["vector"])
                    wait(eng, sPOOL, qkv_grid_readers["gpsimd"])
                if eng == "scalar":
                    qkv_evac[cc] = op(eng, lambda s, o=qkv[:, cc * 192:cc * 192 + 192],
                                      i=bank[:, 0:192]: s.copy(o, i))
                else:
                    qkv_evac[cc] = op(eng, lambda v, o=qkv[:, cc * 192:cc * 192 + 192],
                                      i=bank[:, 0:192]: v.tensor_copy(o, i))
                bank_evac[cc % 2] = (sACT if cc % 2 == 0 else sDVE, qkv_evac[cc])
            qkv_ready = (qkv_evac[CC - 2], qkv_evac[CC - 1])
            ln1f_readers = sPE.n


            if pe_gu is not None:
                pe_gu()

            qkv5 = qkv[:, :].rearrange("p (c s dh dr) -> p s dh c dr",
                                       s=3, dh=NDG, dr=DG)
            P0v = P0[:, :].rearrange("p (e c dr) -> p e c dr", c=CC, dr=DG)
            Ztv = Zt[:, :].rearrange("p (c dr) -> p c dr", dr=DG)
            Wtv = Wt[:, :].rearrange("p (c dr) -> p c dr", dr=DG)
            Wtb = Wtv.unsqueeze(1).broadcast_to([128, CC, CC, DG])
            ctx4 = ctx[:, :].rearrange("p (m dh dr) -> p m dh dr", dh=NDG, dr=DG)

            def qkb(sel, dg):
                if sel == 0:
                    return qkv5[:, 0, dg].unsqueeze(1).broadcast_to([128, CC, CC, DG])
                return qkv5[:, 1, dg].unsqueeze(2).broadcast_to([128, CC, CC, DG])

            wait(E_S, sACT, max(grid_exp_prev, qkv_ready[0]))
            wait(E_S, sDVE, qkv_ready[1])
            s_mark = op(E_S, lambda e, o=P0v, a=qkb(0, 0), b=qkb(1, 0):
                        e.tensor_tensor(o, a, b, ALU.mult))
            for dg in range(NDG):
                pp = dg % 2
                P1t = P1[pp]
                P1v = P1t[:, :].rearrange("p (e c dr) -> p e c dr", c=CC, dr=DG)
                P1z = P1t[:, :].rearrange("p (e c dr) -> p c dr e", c=CC, dr=DG)
                P1m = P1t[:, :].rearrange("p (e c dr) -> p e dr c", c=CC, dr=DG)
                vv = qkv5[:, 2, dg]
                wait("scalar", ESEM[E_S], s_mark)
                wait("scalar", sDVE, grid_ctx_par[pp])
                exp_mark = op("scalar", lambda s, o=P1t[:, :], i=P0[:, :]:
                              s.activation(o, i, AF.Exp, scale=0.125,
                                           bias=mnegT[:, :]))
                grid_exp_prev = exp_mark
                if dg + 1 < NDG:
                    wait(E_S, sACT, exp_mark)
                    s_mark = op(E_S, lambda e, o=P0v, a=qkb(0, dg + 1), b=qkb(1, dg + 1):
                                e.tensor_tensor(o, a, b, ALU.mult))
                wait("vector", sACT, exp_mark)
                op("vector", lambda v, o=zsc[:, 0:2048], a=P1t[:, 0:2048],
                   b=P1t[:, 2048:4096]: v.tensor_tensor(o, a, b, ALU.add))
                op("vector", lambda v, o=zsc[:, 0:1024], a=zsc[:, 0:1024],
                   b=zsc[:, 1024:2048]: v.tensor_tensor(o, a, b, ALU.add))
                zv8 = zsc[:, 0:1024].rearrange("p (e c dr) -> p c dr e", c=CC, dr=DG)
                z_mark = op("vector", lambda v, o=Ztv, i=zv8:
                            v.tensor_reduce(o, i, AX.X, ALU.add))
                wait("vector", sDVE, z_mark)
                rz_done = op("vector", lambda v, o=Zt[:, :], i=Zt[:, :]:
                             v.reciprocal(o, i))
                wait("vector", sDVE, rz_done)
                wdone = op("vector", lambda v, o=Wtv, a=vv, b=Ztv:
                           v.tensor_tensor(o, a, b, ALU.mult))
                gdone = op("vector", lambda e, o=P1v, a=P1v, b=Wtb:
                           e.tensor_tensor(o, a, b, ALU.mult))
                if dg == 0:
                    wait("vector", sPE, ctx_free)
                grid_ctx_par[pp] = op("vector", lambda v, o=ctx4[:, :, dg, :], i=P1m:
                                      v.tensor_reduce(o, i, AX.X, ALU.add))
                grid_ctx_prev = grid_ctx_par[pp]
                if dg == 6 and pe_down is not None:
                    pe_down()
                drain_tick()
                drain_tick()
            qkv_grid_readers = {"vector": sDVE.n, "gpsimd": sPOOL.n}

            wait("tensor", sDVE, grid_ctx_prev)

            def _w_ctx(b, pbf):
                return op("scalar", lambda s, o=ctxf[:, b * 1024:(b + 1) * 1024],
                          i=pbf: s.copy(o, i))
            ct_pe, ct_evac = emit_transpose16(ctx, _w_ctx,
                                              pre_vec_wait=(sPE, ctxf_readers))
            ctx_free = ct_pe

            wait("tensor", sACT, ct_evac)
            y_evac = {}
            for m in range(CC):
                bank = psum[6 + (m % 2)]
                if m >= 2:
                    wait("tensor", sACT if m % 2 == 0 else sDVE, y_evac[m - 2])
                ph, pc = m % 2, m // 2
                op("tensor", lambda t, o=bank[:, 0:64],
                   l=ctxf[ph * 64:ph * 64 + 64, pc * 128:pc * 128 + 128],
                   r=wo_s[ph * 64:ph * 64 + 64, :]:
                   t.matmul(o, l, r, start=True, stop=False))
                mm = op("tensor", lambda t, o=bank[:, 0:64]:
                        t.matmul(o, ones[0:1, :], bo_s[0:1, :], start=False, stop=True))
                eng = "scalar" if m % 2 == 0 else "vector"
                wait(eng, sPE, mm)
                if m < 2:
                    wait(eng, sDVE, yb_free)
                if eng == "scalar":
                    y_evac[m] = op(eng, lambda s, o=yb[:, m * 64:m * 64 + 64],
                                   i=bank[:, 0:64]: s.copy(o, i))
                else:
                    y_evac[m] = op(eng, lambda v, o=yb[:, m * 64:m * 64 + 64],
                                   i=bank[:, 0:64]: v.tensor_copy(o, i))
                bank_evac[m % 2] = (sACT if m % 2 == 0 else sDVE, y_evac[m])
            y_ready = (y_evac[CC - 2], y_evac[CC - 1])
            ctxf_readers = sPE.n

            wait("vector", sACT, y_ready[0])
            wait("vector", sDVE, y_ready[1])
            wait("scalar", sACT, y_ready[0])
            wait("scalar", sDVE, y_ready[1])
            wait("scalar", sPE, lnsc_readers)
            ln2_done = emit_ln(yb[:, :], 1, lnsc[:, :], "ts")
            add1 = op("vector", lambda v, a=xb[:, :], b=yb[:, :]:
                      v.tensor_tensor(a, a, b, ALU.add))
            yb_free = add1

            wait("sync", sDVE, add1)
            xmid_store_done[g] = dma("sync", xmid_d[R:R + SUB, :], xb[:, :], s_xmst)

            wait("scalar", sDVE, add1)
            ln3_done = emit_ln(xb[:, :], 2, lnsc[:, :], "act", lnsc[:, :])
            xb_last_act[g % 2] = ln3_done

            wait("tensor", sACT, ln3_done)

            def _w_ln3(b, pbf, ss=ss):
                dst = ln3f[par][:, :].rearrange("p (j t) -> p j t", t=S)[
                    :, b * 8:(b + 1) * 8, ss * 128:(ss + 1) * 128]
                srcv = pbf.rearrange("p (j t) -> p j t", t=128)
                return op("scalar", lambda s, o=dst, i=srcv: s.copy(o, i))
            t3_pe, _ = emit_transpose16(lnsc, _w_ln3,
                                        pre_vec_wait=(sPE, ln3_readers[par]))
            lnsc_readers = t3_pe
            while ticks:
                drain_tick()

        def emit_ffn_macro(sup, mac):
            par = sup % 2
            ln3_rhs = ln3f[par]
            ticks = []
            prev_down_mark = gated_free
            nstep_base = (sup * NMAC + mac) * NSTEP

            def pe_gu():
                for st_ in range(NSTEP):
                    s = mac * NSTEP + st_
                    gstep = nstep_base + st_
                    p = s % 2
                    wait("sync", s_cT, conv_target(mac))
                    wait("sync", sPE, wgt_free[p])
                    gv = scr_gT.rearrange("(j q) H -> q j H", q=128)
                    uv = scr_uT.rearrange("(j q) H -> q j H", q=128)
                    for q4 in range(4):
                        dma("sync",
                            wgT[p][:, q4 * 512:(q4 + 1) * 512].rearrange("p (j h) -> p j h", h=128),
                            gv[:, q4 * 4:(q4 + 1) * 4, s * 128:(s + 1) * 128], s_w[p])
                    for q4 in range(4):
                        dma("sync",
                            wuT[p][:, q4 * 512:(q4 + 1) * 512].rearrange("p (j h) -> p j h", h=128),
                            uv[:, q4 * 4:(q4 + 1) * 4, s * 128:(s + 1) * 128], s_w[p])
                    wait("tensor", s_w[p], s_w[p].n)
                    if mac == 0 and st_ == 0:
                        wait("tensor", sACT, ln3_ready[par])
                    wait("tensor", s_silu, gstep)
                    gate_done = 0
                    for j in range(16):
                        gate_done = op("tensor", lambda t, o=psum[0][:, :],
                                       l=wgT[p][:, j * 128:(j + 1) * 128],
                                       r=ln3_rhs[:, j * S:(j + 1) * S],
                                       st=(j == 0), sp=(j == 15):
                                       t.matmul(o, l, r, start=st, stop=sp))
                    wait("tensor", s_gated, gstep)
                    up_done = 0
                    for j in range(16):
                        up_done = op("tensor", lambda t, o=psum[1][:, :],
                                     l=wuT[p][:, j * 128:(j + 1) * 128],
                                     r=ln3_rhs[:, j * S:(j + 1) * S],
                                     st=(j == 0), sp=(j == 15):
                                     t.matmul(o, l, r, start=st, stop=sp))
                    wgt_free[p] = sPE.n

                    def tick_step(st_=st_, gstep=gstep, gate_done=gate_done,
                                  up_done=up_done):
                        wait("scalar", sPE, gate_done)
                        wait("scalar", s_gated, gstep)
                        th = op("scalar", lambda sc: sc.activation(
                            silu_sb[:, :], psum[0][:, :], AF.Tanh, scale=0.5))
                        op_det("scalar", lambda sc: sc.copy(
                            gate_sb[:, :], psum[0][:, :]), s_silu)
                        wait("vector", sACT, th)
                        op("vector", lambda v: v.scalar_tensor_tensor(
                            silu_sb[:, :], silu_sb[:, :], 1.0, gate_sb[:, :],
                            ALU.add, ALU.mult))
                        wait("vector", sPE, up_done)
                        if st_ == 0:
                            wait("vector", sPE, prev_down_mark)
                        op_det("vector", lambda v, o=gated[:, st_ * S:(st_ + 1) * S]:
                               v.scalar_tensor_tensor(o, silu_sb[:, :], 0.5,
                                                      psum[1][:, :], ALU.mult, ALU.mult),
                               s_gated)
                    ticks.append(tick_step)

            def pe_down():
                nonlocal gated_free
                for d4 in range(4):
                    dn = [0, 0, 0, 0]
                    for mc in range(NSTEP):
                        p = mc % 2
                        wait("sync", sPE, wdt_free[p])
                        dv = scr_dT.rearrange("(c q) D -> q c D", q=128)
                        dma("sync", wdT[p][:, :],
                            dv[:, mac * NSTEP + mc, d4 * 512:(d4 + 1) * 512], s_wd[p])
                        wait("tensor", s_wd[p], s_wd[p].n)
                        if mc == 0:
                            wait("tensor", s_gated, nstep_base + NSTEP)
                            wait("tensor", s_dnev, (sup * NMAC + mac) * 16 + d4 * 4)
                        for tt in range(4):
                            dn[tt] = op("tensor", lambda t, o=psum[2 + tt][:, :],
                                        l=gated[:, mc * S + tt * 128:mc * S + (tt + 1) * 128],
                                        r=wdT[p][:, :],
                                        st=(mc == 0), sp=(mc == NSTEP - 1):
                                        t.matmul(o, l, r, start=st, stop=sp))
                        wdt_free[p] = sPE.n

                    def mk_evac(d4=d4, dn=tuple(dn)):
                        def tick_evac(tt, mark):
                            wait("vector", sPE, mark)
                            aslice = acc[:, tt * DM + d4 * 512: tt * DM + (d4 + 1) * 512]
                            if mac == 0:
                                op_det("vector", lambda v, o=aslice, i=psum[2 + tt][:, :]:
                                       v.tensor_copy(o, i), s_dnev)
                            else:
                                op_det("vector", lambda v, o=aslice, i=psum[2 + tt][:, :]:
                                       v.tensor_tensor(o, o, i, ALU.add), s_dnev)
                        for tt in range(4):
                            ticks.append(lambda tt=tt, dn=dn, f=tick_evac: f(tt, dn[tt]))
                    mk_evac()
                gated_free = sPE.n

            return pe_gu, pe_down, ticks

        def emit_ffn_tail(sup):
            nonlocal xm_free
            par = sup % 2
            ln3_readers[par] = sPE.n
            wait("vector", s_dnev, (sup + 1) * NMAC * 16)
            wait("scalar", s_dnev, (sup + 1) * NMAC * 16)
            wait("scalar", sPE, lnsc_readers)
            c = lambda t, i: stt[:, t * 8 + i:t * 8 + i + 1]
            asl = [acc[:, tt * DM:(tt + 1) * DM] for tt in range(NSUB)]
            for tt in range(NSUB):
                op("vector", lambda v, o=c(tt, 0), x=asl[tt]: v.reduce_sum(o, x, AX.X))
            sq_last = 0
            for tt in range(NSUB):
                sq_last = op("scalar", lambda s, o=lnsc[:, :], x=asl[tt], a=c(tt, 1):
                             s.activation(o, x, AF.Square, accum_out=a))
            wait("vector", sACT, sq_last)
            m2_last = 0
            for tt in range(NSUB):
                m2_last = op("vector", lambda v, o=c(tt, 2), i0=c(tt, 0):
                             v.tensor_scalar(o, i0, i0, inv_n2, ALU.mult, ALU.mult))
            wait("vector", sDVE, m2_last)
            ve_last = 0
            for tt in range(NSUB):
                ve_last = op("vector", lambda v, o=c(tt, 3), i0=c(tt, 1), i1=c(tt, 2):
                             v.scalar_tensor_tensor(o, i0, inv_n, i1, ALU.mult, ALU.subtract))
            wait("scalar", sDVE, ve_last)
            for tt in range(NSUB):
                op("scalar", lambda s, o=c(tt, 6), i=c(tt, 3):
                   s.activation(o, i, AF.Ln, bias=epsT[:, :]))
            r_last = 0
            for tt in range(NSUB):
                r_last = op("scalar", lambda s, o=c(tt, 4), i=c(tt, 6):
                            s.activation(o, i, AF.Exp, scale=-0.5))
            wait("vector", sACT, r_last)
            b_last = 0
            for tt in range(NSUB):
                b_last = op("vector", lambda v, o=c(tt, 5), i0=c(tt, 0), r=c(tt, 4):
                            v.tensor_scalar(o, i0, r, -inv_n, ALU.mult, ALU.mult))
            wait("vector", sDVE, b_last)
            for tt in range(NSUB):
                op("vector", lambda v, x=asl[tt], r=c(tt, 4), b=c(tt, 5):
                   v.tensor_scalar(x, x, r, b, ALU.mult, ALU.add))
            for tt in range(NSUB):
                g = sup * NSUB + tt
                R = g * SUB
                wait("sync", s_xmst, xmid_store_done[g])
                wait("sync", sDVE, xm_free)
                wait("sync", sACT, max(xb_last_act))
                wait("sync", s_out, s_out.n)
                xml = dma("sync", xm[:, :], xmid_d[R:R + SUB, :], s_x)
                wait("vector", s_x, xml)
                addf = op("vector", lambda v, a=xm[:, :], b=asl[tt]:
                          v.tensor_tensor(a, a, b, ALU.add))
                xm_free = addf
                wait("sync", sDVE, addf)
                dma("sync", out_t[R:R + SUB, :], xm[:, :], s_out)

        for _grp in range(4):
            emit_conv_group(_grp)
        x_load_done[0] = dma("sync", xbuf[0][:, :], x_in[0:SUB, :], s_x)
        x_load_done[1] = dma("sync", xbuf[1][:, :], x_in[SUB:2 * SUB, :], s_x)
        x_load_done[2] = dma("sync", xm[:, :], x_in[2 * SUB:3 * SUB, :], s_x)
        for _g in range(NSUB):
            emit_attn_subtile(_g)
        ln3_ready[0] = sACT.n
        convT_total_mark[0] = s_cT.n
        for sup in range(NSUP):
            for k in range(NMAC):
                pe_gu, pe_down, ticks = emit_ffn_macro(sup, k)
                if sup + 1 < NSUP:
                    emit_attn_subtile((sup + 1) * NSUB + k, pe_gu, pe_down, ticks)
                else:
                    pe_gu()
                    while ticks[:16]:
                        ticks.pop(0)()
                    pe_down()
                    while ticks:
                        ticks.pop(0)()
            if sup + 1 < NSUP:
                ln3_ready[(sup + 1) % 2] = sACT.n
            emit_ffn_tail(sup)
        if debug:
            wait("sync", sDVE, sDVE.n)
            wait("sync", sACT, sACT.n)
            wait("sync", sPOOL, sPOOL.n)
            wait("sync", sPE, sPE.n)
            for nm, sbuf in (("qkv", qkv), ("ctx", ctx), ("ctxf", ctxf), ("yb", yb),
                             ("xmid_last", xbuf[1]), ("ln1f", ln1f), ("lnsc", lnsc),
                             ("Zt", Zt), ("Wt", Wt), ("P0", P0),
                             ("P1", P1[0]), ("acc", acc), ("stt", stt), ("gated", gated),
                             ("wgT", wgT[1])):
                dma("sync", dbg[nm], sbuf[:, :], s_out)
            dma("sync", dbg["ln3f"], ln3f[1][:, :], s_out)
            dma("sync", dbg["xmid_all"], xmid_d, s_out)
        wait("sync", s_out, s_out.n)
        wait("sync", s_xmst, s_xmst.n)
        wait("sync", s_conv, s_conv.n)

        with nc.allow_low_precision(reason="bf16 grid/ctx intermediates"), \
                nc.Block() as block:
            @block.sync
            def _(e):
                for f in sched["sync"]:
                    f(e)

            @block.gpsimd
            def _(e):
                for f in sched["gpsimd"]:
                    f(e)

            @block.tensor
            def _(e):
                for f in sched["tensor"]:
                    f(e)

            @block.vector
            def _(e):
                for f in sched["vector"]:
                    f(e)

            @block.scalar
            def _(e):
                for f in sched["scalar"]:
                    f(e)

    return nc


_NC_CACHE = [None]


def kernel(L0_attn_x, L0_attn_Wqkv, L0_attn_bqkv, L0_attn_Wout, L0_attn_bout,
           L0_ffn_w_gate, L0_ffn_w_up, L0_ffn_w_down):
    x = np.ascontiguousarray(np.asarray(L0_attn_x, dtype=np.float32)).reshape(16384, 2048)
    wqkv = np.ascontiguousarray(np.asarray(L0_attn_Wqkv, dtype=np.float32))
    bqkv = np.ascontiguousarray(np.asarray(L0_attn_bqkv, dtype=np.float32))
    wout = np.ascontiguousarray(np.asarray(L0_attn_Wout, dtype=np.float32))
    bout = np.ascontiguousarray(np.asarray(L0_attn_bout, dtype=np.float32))
    wg = np.ascontiguousarray(np.asarray(L0_ffn_w_gate, dtype=np.float32))
    wu = np.ascontiguousarray(np.asarray(L0_ffn_w_up, dtype=np.float32))
    wd = np.ascontiguousarray(np.asarray(L0_ffn_w_down, dtype=np.float32))

    if _NC_CACHE[0] is None:
        _NC_CACHE[0] = build_program()
    nc = _NC_CACHE[0]

    in_maps = []
    for i in range(NCORES):
        in_maps.append({
            "x": x[i * TOK:(i + 1) * TOK],
            "wqkv": wqkv, "bqkv": bqkv, "wout": wout, "bout": bout,
            "wg": wg, "wu": wu, "wd": wd,
        })
    res = run_bass_kernel_spmd(nc, in_maps, core_ids=list(range(NCORES)))
    out = np.concatenate([np.asarray(r["out"]) for r in res.results], axis=0)
    return out.reshape(4, 4096, 2048)


# version 18
# speedup vs baseline: 1.0005x; 1.0005x over previous
import contextlib
import numpy as np
import concourse.bass as bass
import concourse.mybir as mybir
from concourse.bass_utils import run_bass_kernel_spmd

F32 = mybir.dt.float32
BF16 = mybir.dt.bfloat16
AF = mybir.ActivationFunctionType
ALU = mybir.AluOpType
AX = mybir.AxisListType

NCORES = 8
TOK = 2048
DM = 2048
CC, KH = 32, 64
HID = 8192
S = 512
NSUP = TOK // S
SUB = 128
NSUB = S // SUB
DG = 4
NDG = KH // DG
MACRO = 2048
NMAC = HID // MACRO
NSTEP = MACRO // 128
EPS = 1e-5
POOL_GRID_FRAC = 2


def build_program(debug=False):
    nc = bass.Bass("TRN2", target_bir_lowering=False)

    x_in = nc.dram_tensor("x", [TOK, DM], F32, kind="ExternalInput").ap()
    wqkv = nc.dram_tensor("wqkv", [CC, KH, 3 * KH], F32, kind="ExternalInput").ap()
    bqkv = nc.dram_tensor("bqkv", [CC, 3 * KH], F32, kind="ExternalInput").ap()
    wout = nc.dram_tensor("wout", [KH, KH], F32, kind="ExternalInput").ap()
    bout = nc.dram_tensor("bout", [KH], F32, kind="ExternalInput").ap()
    wg = nc.dram_tensor("wg", [HID, DM], F32, kind="ExternalInput").ap()
    wu = nc.dram_tensor("wu", [HID, DM], F32, kind="ExternalInput").ap()
    wd = nc.dram_tensor("wd", [DM, HID], F32, kind="ExternalInput").ap()
    out_t = nc.dram_tensor("out", [TOK, DM], F32, kind="ExternalOutput").ap()

    dbg = {}
    if debug:
        def dbg_t(name, shape, dt):
            dbg[name] = nc.dram_tensor("dbg_" + name, shape, dt, kind="ExternalOutput").ap()
        dbg_t("qkv", [128, CC * 192], BF16)
        dbg_t("ctx", [128, DM], BF16)
        dbg_t("ctxf", [128, DM], BF16)
        dbg_t("yb", [128, DM], F32)
        dbg_t("xmid_last", [128, DM], F32)
        dbg_t("ln1f", [128, DM], BF16)
        dbg_t("lnsc", [128, DM], BF16)
        dbg_t("Zt", [128, 32 * DG], F32)
        dbg_t("Wt", [128, 32 * DG], BF16)
        dbg_t("P0", [128, 2048], BF16)
        dbg_t("P1", [128, 2048], BF16)
        dbg_t("acc", [128, NSUB * DM], F32)
        dbg_t("ln3f", [128, 16 * S], BF16)
        dbg_t("stt", [128, 32], F32)
        dbg_t("gated", [128, NSTEP * S], BF16)
        dbg_t("xmid_all", [TOK, DM], F32)
        dbg_t("wgT", [128, 2048], BF16)

    wg_bf = nc.dram_tensor("wg_bf", [HID, DM], BF16, kind="Internal").ap()
    wu_bf = nc.dram_tensor("wu_bf", [HID, DM], BF16, kind="Internal").ap()
    wd_bf = nc.dram_tensor("wd_bf", [DM, HID], BF16, kind="Internal").ap()
    scr_gT = nc.dram_tensor("scr_gT", [DM, HID], BF16, kind="Internal").ap()
    scr_uT = nc.dram_tensor("scr_uT", [DM, HID], BF16, kind="Internal").ap()
    scr_dT = nc.dram_tensor("scr_dT", [HID, DM], BF16, kind="Internal").ap()
    xmid_d = nc.dram_tensor("xmid", [TOK, DM], F32, kind="Internal").ap()

    stk = contextlib.ExitStack()
    with stk:
        def sb(name, shape, dt):
            return stk.enter_context(nc.sbuf_tensor(name, shape, dt))

        acc = sb("acc", [128, NSUB * DM], F32)
        ln3f = [sb("ln3a", [128, 16 * S], BF16), sb("ln3b", [128, 16 * S], BF16)]
        wgT = [sb("wgT0", [128, 2048], BF16), sb("wgT1", [128, 2048], BF16)]
        wuT = [sb("wuT0", [128, 2048], BF16), sb("wuT1", [128, 2048], BF16)]
        wdT = [sb("wdT0", [128, 512], BF16), sb("wdT1", [128, 512], BF16)]
        gated = sb("gated", [128, NSTEP * S], BF16)
        silu_sb = sb("silu", [128, 512], BF16)
        gate_sb = sb("gate_sb", [128, 512], BF16)
        qkv = sb("qkv", [128, CC * 192], BF16)
        P0 = sb("P0", [128, 1024 * DG], BF16)
        P1 = [sb("P1a", [128, 1024 * DG], BF16), sb("P1b", [128, 1024 * DG], BF16)]
        Zt = sb("Zt", [128, 32 * DG], F32)
        Wt = sb("Wt", [128, 32 * DG], BF16)
        ctx = sb("ctx", [128, DM], BF16)
        ctxf = sb("ctxf", [128, DM], BF16)
        yb = sb("yb", [128, DM], BF16)
        lnsc = sb("lnsc", [128, DM], BF16)
        ln1f = sb("ln1f", [128, DM], BF16)
        xbuf = [sb("xa", [128, DM], F32), sb("xb", [128, DM], F32)]
        xm = sb("xm", [128, DM], F32)
        wq_s = sb("wq", [128, 16 * 192], BF16)
        wo_s = sb("wo", [128, 64], BF16)
        bq_s = sb("bq", [1, CC * 192], BF16)
        bo_s = sb("bo", [1, 64], BF16)
        ones = sb("ones", [1, 128], BF16)
        ident = sb("ident", [128, 128], BF16)
        stt = sb("stats", [128, 32], F32)
        epsT = sb("epsT", [128, 1], F32)
        mnegT = sb("mnegT", [128, 1], F32)
        zsc = sb("zsc", [128, 2048], BF16)

        import os as _os
        if _os.environ.get("SBUF_DEBUG"):
            print("SBUF remaining after allocs:", nc.sbuf_bytes_remaining)
        psum = [stk.enter_context(nc.psum_tensor(f"ps{i}", [128, 512], F32))
                for i in range(8)]

        class Sem:
            __slots__ = ("h", "n")

            def __init__(self, name):
                self.h = stk.enter_context(nc.semaphore(name))
                self.n = 0

        s_x = Sem("s_x")
        s_xmst = Sem("s_xmst")
        s_out = Sem("s_out")
        s_attw = Sem("s_attw")
        s_conv = Sem("s_conv")
        s_w = [Sem("s_w0"), Sem("s_w1")]
        s_cT = Sem("s_cT")
        s_silu = Sem("s_silu")
        s_gated = Sem("s_gated")
        s_dnev = Sem("s_dnev")
        s_wd = [Sem("s_wd0"), Sem("s_wd1")]
        sPE = Sem("sPE")
        sDVE = Sem("sDVE")
        sACT = Sem("sACT")
        sPOOL = Sem("sPOOL")
        ESEM = {"tensor": sPE, "vector": sDVE, "scalar": sACT, "gpsimd": sPOOL}

        sched = {k: [] for k in ("sync", "gpsimd", "tensor", "vector", "scalar")}

        def em(eng, fn):
            sched[eng].append(fn)

        def wait(eng, sem, val=None):
            v = sem.n if val is None else val
            if v > 0:
                em(eng, lambda e, h=sem.h, vv=v: e.wait_ge(h, vv))

        def op(eng, fn):
            se = ESEM[eng]
            se.n += 1
            em(eng, lambda e, f=fn, h=se.h: f(e).then_inc(h, 1))
            return se.n

        def op_det(eng, fn, sem):
            sem.n += 1
            em(eng, lambda e, f=fn, h=sem.h: f(e).then_inc(h, 1))
            return sem.n

        def dma(eng, dst, src, sem, transpose=False):
            sem.n += 16
            em(eng, lambda e, d=dst, s=src, h=sem.h, t=transpose:
               e.dma_start(d, s, transpose=t).then_inc(h, 16))
            return sem.n

        mz = op("gpsimd", lambda g: g.memset(ident[:, :], 0.0))
        wait("gpsimd", sPOOL, mz)
        op("gpsimd", lambda g: g.affine_select(
            out=ident[:, :], in_=ident[:, :], compare_op=ALU.not_equal,
            fill=1.0, base=0, pattern=[[-1, 128]], channel_multiplier=1))
        op("gpsimd", lambda g: g.memset(ones[:, :], 1.0))
        op("gpsimd", lambda g: g.memset(epsT[:, :], EPS))
        pool_consts = op("gpsimd", lambda g: g.memset(mnegT[:, :], -2.0))

        wq_re = (wqkv.rearrange("(cp ch) k i -> cp ch k i", ch=2)
                 .transpose([1, 2, 0, 3])
                 .rearrange("ch k cp i -> (ch k) cp i"))
        wq_dst = wq_s[:, :].rearrange("p (cp i) -> p cp i", i=192)
        for j in range(4):
            dma("gpsimd", wq_dst[:, j * 4:(j + 1) * 4, :],
                wq_re[:, j * 4:(j + 1) * 4, :], s_attw)
        dma("gpsimd", wo_s[0:64, :], wout, s_attw)
        dma("gpsimd", wo_s[64:128, :], wout, s_attw)
        dma("gpsimd", bq_s[0:1, :], bqkv.rearrange("c i -> (c i)").unsqueeze(0), s_attw)
        dma("gpsimd", bo_s[0:1, :], bout.unsqueeze(0), s_attw)
        attw_done = s_attw.n
        prologue_hooks = []

        conv_groups = []
        for grp in range(4):
            rows = slice(grp * 2048, (grp + 1) * 2048)
            grp_dmas = [(wg[rows, :], wg_bf[rows, :], False),
                        (wu[rows, :], wu_bf[rows, :], False)]
            for q in range(4):
                r2 = slice(q * 512, (q + 1) * 512)
                grp_dmas.append((wd[r2, grp * 2048:(grp + 1) * 2048],
                                 wd_bf[r2, grp * 2048:(grp + 1) * 2048], False))
            conv_groups.append(grp_dmas)
        conv_emitted = [0]

        def emit_conv_group(grp):
            if conv_emitted[0] > grp:
                return
            conv_emitted[0] = grp + 1
            for s_ap, d_ap, _ in conv_groups[grp]:
                dma("gpsimd", d_ap, s_ap, s_conv)

        def conv_cast_target(grp):
            return 16 * 6 * (grp + 1)

        convT_units = []
        for grp in range(4):
            for j in range(16):
                convT_units.append((wg_bf[grp * 2048:(grp + 1) * 2048, j * 128:(j + 1) * 128],
                                    scr_gT[j * 128:(j + 1) * 128, grp * 2048:(grp + 1) * 2048], grp))
            for j in range(16):
                convT_units.append((wu_bf[grp * 2048:(grp + 1) * 2048, j * 128:(j + 1) * 128],
                                    scr_uT[j * 128:(j + 1) * 128, grp * 2048:(grp + 1) * 2048], grp))
            for hc in range(16):
                h0 = grp * 2048 + hc * 128
                convT_units.append((wd_bf[0:2048, h0:h0 + 128],
                                    scr_dT[h0:h0 + 128, :], grp))
        convT_next = [0]
        convT_store_done = {}
        convT_group_mark = {}

        def emit_convT(nunits, convT_bufs):
            for _ in range(nunits):
                u = convT_next[0]
                if u >= len(convT_units):
                    return
                srcb, dstb, grp = convT_units[u]
                buf = convT_bufs[u % 2]
                wait("sync", s_conv, conv_cast_target(grp))
                if u >= 2:
                    wait("sync", s_cT, convT_store_done[u - 2])
                t = dma("sync", buf, srcb, s_cT, transpose=True)
                wait("sync", s_cT, t)
                convT_store_done[u] = dma("sync", dstb, buf, s_cT)
                convT_group_mark[grp] = convT_store_done[u]
                convT_next[0] = u + 1

        def conv_target(grp):
            return convT_group_mark[grp]

        inv_n = 1.0 / DM
        inv_n2 = inv_n * inv_n

        def emit_ln(xap, set_i, sq_out, apply_kind, apply_out=None):
            base = set_i * 8
            c = lambda i: stt[:, base + i:base + i + 1]
            su_done = op("vector", lambda v, o=c(0), x=xap: v.reduce_sum(o, x, AX.X))
            sq = op("scalar", lambda s, o=sq_out, x=xap, a=c(1):
                    s.activation(o, x, AF.Square, accum_out=a))
            wait("vector", sACT, sq)
            wait("vector", sDVE, su_done)
            m2 = op("vector", lambda v, o=c(2), i0=c(0):
                    v.tensor_scalar(o, i0, i0, inv_n2, ALU.mult, ALU.mult))
            wait("vector", sDVE, m2)
            ve = op("vector", lambda v, o=c(3), i0=c(1), i1=c(2):
                    v.scalar_tensor_tensor(o, i0, inv_n, i1, ALU.mult, ALU.subtract))
            wait("scalar", sDVE, ve)
            op("scalar", lambda s, o=c(6), i=c(3): s.activation(o, i, AF.Ln, bias=epsT[:, :]))
            r_done = op("scalar", lambda s, o=c(4), i=c(6):
                        s.activation(o, i, AF.Exp, scale=-0.5))
            wait("vector", sACT, r_done)
            b_done = op("vector", lambda v, o=c(5), i0=c(0), r=c(4):
                        v.tensor_scalar(o, i0, r, -inv_n, ALU.mult, ALU.mult))
            if apply_kind == "act":
                wait("scalar", sDVE, b_done)
                wait("scalar", sACT, r_done)
                done = op("scalar", lambda s, o=apply_out, x=xap, r=c(4), b=c(5):
                          s.activation(o, x, AF.Identity, bias=b, scale=r))
            else:
                wait("vector", sDVE, b_done)
                done = op("vector", lambda v, x=xap, r=c(4), b=c(5):
                          v.tensor_scalar(x, x, r, b, ALU.mult, ALU.add))
            return done

        bank_evac = [(sACT, 0), (sACT, 0)]

        def emit_transpose16(src_sb, dst_fn, pre_vec_wait=None):
            last_pe = 0
            for b in range(2):
                bank = psum[6 + b]
                pbf = bank[:, :].bitcast(BF16)
                wait("tensor", bank_evac[b][0], bank_evac[b][1])
                wait("tensor", sPOOL, pool_consts)
                for jj in range(8):
                    j = b * 8 + jj
                    last_pe = op("tensor", lambda t, o=pbf[:, jj * 128:(jj + 1) * 128],
                                 i=src_sb[:, j * 128:(j + 1) * 128]:
                                 t.transpose(o, i, ident[:, :]))
                wait("scalar", sPE, last_pe)
                if pre_vec_wait is not None and b == 0:
                    wait("scalar", pre_vec_wait[0], pre_vec_wait[1])
                bank_evac[b] = (sACT, dst_fn(b, pbf))
            return last_pe, bank_evac[1][1]

        x_load_done = [0] * (TOK // SUB)
        xmid_store_done = [0] * (TOK // SUB)
        ln3_ready = [0, 0]
        ln3_readers = [0, 0]
        wgt_free = [0, 0]
        wdt_free = [0, 0]
        qkv_grid_readers = {"vector": 0, "gpsimd": 0}
        lnsc_readers = 0
        ln1f_readers = 0
        ctxf_readers = 0
        xb_last_act = [0, 0]
        yb_free = ("vector", 0)
        ctx_free = 0
        gated_free = 0
        grid_exp_prev = 0
        grid_ctx_prev = 0
        grid_ctx_par = [0, 0]
        grid_z_prev_mark = [0]
        xm_free = 0

        wait("scalar", sPOOL, pool_consts)
        convT_total_mark = [0]
        def emit_attn_subtile(g, pe_gu=None, pe_down=None, ticks=None):
            ticks = ticks if ticks is not None else []

            def drain_tick():
                if ticks:
                    ticks.pop(0)()
            nonlocal lnsc_readers, ln1f_readers, ctxf_readers, yb_free, ctx_free
            nonlocal grid_exp_prev, grid_ctx_prev, qkv_grid_readers, xm_free
            sup = g // NSUB
            par = sup % 2
            ss = g % NSUB
            if g == NSUB:
                wait("scalar", s_cT, convT_total_mark[0])
            R = g * SUB
            xb = xbuf[g % 2]
            E_S = "vector"
            E_M = "vector"
            if g == 2:
                xb = xm

            if g >= 3:
                wait("sync", s_xmst, xmid_store_done[g - 2])
                wait("sync", sACT, xb_last_act[g % 2])
                x_load_done[g] = dma("sync", xb[:, :], x_in[R:R + SUB, :], s_x)
            if g < NSUB:
                convT_bufs = [ln3f[1][:, 0:2048], ln3f[1][:, 2048:4096]]
                emit_convT(48, convT_bufs)

            wait("vector", s_x, x_load_done[g])
            wait("scalar", s_x, x_load_done[g])
            wait("scalar", sPE, lnsc_readers)
            ln1_done = emit_ln(xb[:, :], 0, lnsc[:, :], "act", lnsc[:, :])

            wait("tensor", sACT, ln1_done)

            def _w_ln1(b, pbf):
                return op("scalar", lambda s, o=ln1f[:, b * 1024:(b + 1) * 1024],
                          i=pbf: s.copy(o, i))
            t1_pe, t1_evac = emit_transpose16(lnsc, _w_ln1,
                                              pre_vec_wait=(sPE, ln1f_readers))
            lnsc_readers = t1_pe

            wait("tensor", s_attw, attw_done)
            wait("tensor", sACT, t1_evac)
            qkv_evac = {}
            for cc in range(CC):
                bank = psum[6 + (cc % 2)]
                if cc >= 2:
                    wait("tensor", sACT if cc % 2 == 0 else sDVE, qkv_evac[cc - 2])
                ph, pc = cc % 2, cc // 2
                op("tensor", lambda t, o=bank[:, 0:192],
                   l=ln1f[ph * 64:ph * 64 + 64, pc * 128:pc * 128 + 128],
                   r=wq_s[ph * 64:ph * 64 + 64, pc * 192:pc * 192 + 192]:
                   t.matmul(o, l, r, start=True, stop=False))
                mm = op("tensor", lambda t, o=bank[:, 0:192],
                        r=bq_s[0:1, cc * 192:cc * 192 + 192]:
                        t.matmul(o, ones[0:1, :], r, start=False, stop=True))
                eng = "scalar" if cc % 2 == 0 else "vector"
                wait(eng, sPE, mm)
                if cc < 2:
                    wait(eng, sDVE, qkv_grid_readers["vector"])
                    wait(eng, sPOOL, qkv_grid_readers["gpsimd"])
                if eng == "scalar":
                    qkv_evac[cc] = op(eng, lambda s, o=qkv[:, cc * 192:cc * 192 + 192],
                                      i=bank[:, 0:192]: s.copy(o, i))
                else:
                    qkv_evac[cc] = op(eng, lambda v, o=qkv[:, cc * 192:cc * 192 + 192],
                                      i=bank[:, 0:192]: v.tensor_copy(o, i))
                bank_evac[cc % 2] = (sACT if cc % 2 == 0 else sDVE, qkv_evac[cc])
            qkv_ready = (qkv_evac[CC - 2], qkv_evac[CC - 1])
            ln1f_readers = sPE.n


            if pe_gu is not None:
                pe_gu()

            qkv5 = qkv[:, :].rearrange("p (c s dh dr) -> p s dh c dr",
                                       s=3, dh=NDG, dr=DG)
            P0v = P0[:, :].rearrange("p (e c dr) -> p e c dr", c=CC, dr=DG)
            Ztv = Zt[:, :].rearrange("p (c dr) -> p c dr", dr=DG)
            Wtv = Wt[:, :].rearrange("p (c dr) -> p c dr", dr=DG)
            Wtb = Wtv.unsqueeze(1).broadcast_to([128, CC, CC, DG])
            ctx4 = ctx[:, :].rearrange("p (m dh dr) -> p m dh dr", dh=NDG, dr=DG)

            def qkb(sel, dg):
                if sel == 0:
                    return qkv5[:, 0, dg].unsqueeze(1).broadcast_to([128, CC, CC, DG])
                return qkv5[:, 1, dg].unsqueeze(2).broadcast_to([128, CC, CC, DG])

            wait(E_S, sACT, max(grid_exp_prev, qkv_ready[0]))
            wait(E_S, sDVE, qkv_ready[1])
            s_mark = op(E_S, lambda e, o=P0v, a=qkb(0, 0), b=qkb(1, 0):
                        e.tensor_tensor(o, a, b, ALU.mult))
            for dg in range(NDG):
                pp = dg % 2
                P1t = P1[pp]
                P1v = P1t[:, :].rearrange("p (e c dr) -> p e c dr", c=CC, dr=DG)
                P1z = P1t[:, :].rearrange("p (e c dr) -> p c dr e", c=CC, dr=DG)
                P1m = P1t[:, :].rearrange("p (e c dr) -> p e dr c", c=CC, dr=DG)
                vv = qkv5[:, 2, dg]
                wait("scalar", ESEM[E_S], s_mark)
                wait("scalar", sDVE, grid_ctx_par[pp])
                exp_mark = op("scalar", lambda s, o=P1t[:, :], i=P0[:, :]:
                              s.activation(o, i, AF.Exp, scale=0.125,
                                           bias=mnegT[:, :]))
                grid_exp_prev = exp_mark
                if dg + 1 < NDG:
                    wait(E_S, sACT, exp_mark)
                    s_mark = op(E_S, lambda e, o=P0v, a=qkb(0, dg + 1), b=qkb(1, dg + 1):
                                e.tensor_tensor(o, a, b, ALU.mult))
                wait("gpsimd", sACT, exp_mark)
                wait("gpsimd", sDVE, grid_z_prev_mark[0])
                zl1 = op("gpsimd", lambda p, o=zsc[:, 0:2048], a=P1t[:, 0:2048],
                         b=P1t[:, 2048:4096]: p.tensor_tensor(o, a, b, ALU.add))
                wait("vector", sPOOL, zl1)
                op("vector", lambda v, o=zsc[:, 0:1024], a=zsc[:, 0:1024],
                   b=zsc[:, 1024:2048]: v.tensor_tensor(o, a, b, ALU.add))
                zv8 = zsc[:, 0:1024].rearrange("p (e c dr) -> p c dr e", c=CC, dr=DG)
                z_mark = op("vector", lambda v, o=Ztv, i=zv8:
                            v.tensor_reduce(o, i, AX.X, ALU.add))
                grid_z_prev_mark[0] = z_mark
                wait("vector", sDVE, z_mark)
                rz_done = op("vector", lambda v, o=Zt[:, :], i=Zt[:, :]:
                             v.reciprocal(o, i))
                wait("vector", sDVE, rz_done)
                wdone = op("vector", lambda v, o=Wtv, a=vv, b=Ztv:
                           v.tensor_tensor(o, a, b, ALU.mult))
                gdone = op("vector", lambda e, o=P1v, a=P1v, b=Wtb:
                           e.tensor_tensor(o, a, b, ALU.mult))
                if dg == 0:
                    wait("vector", sPE, ctx_free)
                grid_ctx_par[pp] = op("vector", lambda v, o=ctx4[:, :, dg, :], i=P1m:
                                      v.tensor_reduce(o, i, AX.X, ALU.add))
                grid_ctx_prev = grid_ctx_par[pp]
                if dg == 6 and pe_down is not None:
                    pe_down()
                drain_tick()
                drain_tick()
            qkv_grid_readers = {"vector": sDVE.n, "gpsimd": sPOOL.n}

            wait("tensor", sDVE, grid_ctx_prev)

            def _w_ctx(b, pbf):
                return op("scalar", lambda s, o=ctxf[:, b * 1024:(b + 1) * 1024],
                          i=pbf: s.copy(o, i))
            ct_pe, ct_evac = emit_transpose16(ctx, _w_ctx,
                                              pre_vec_wait=(sPE, ctxf_readers))
            ctx_free = ct_pe

            wait("tensor", sACT, ct_evac)
            y_evac = {}
            for m in range(CC):
                bank = psum[6 + (m % 2)]
                if m >= 2:
                    wait("tensor", sACT if m % 2 == 0 else sDVE, y_evac[m - 2])
                ph, pc = m % 2, m // 2
                op("tensor", lambda t, o=bank[:, 0:64],
                   l=ctxf[ph * 64:ph * 64 + 64, pc * 128:pc * 128 + 128],
                   r=wo_s[ph * 64:ph * 64 + 64, :]:
                   t.matmul(o, l, r, start=True, stop=False))
                mm = op("tensor", lambda t, o=bank[:, 0:64]:
                        t.matmul(o, ones[0:1, :], bo_s[0:1, :], start=False, stop=True))
                eng = "scalar" if m % 2 == 0 else "vector"
                wait(eng, sPE, mm)
                if m < 2:
                    wait(eng, ESEM[yb_free[0]] if isinstance(yb_free, tuple) else sDVE,
                         yb_free[1] if isinstance(yb_free, tuple) else yb_free)
                if eng == "scalar":
                    y_evac[m] = op(eng, lambda s, o=yb[:, m * 64:m * 64 + 64],
                                   i=bank[:, 0:64]: s.copy(o, i))
                else:
                    y_evac[m] = op(eng, lambda v, o=yb[:, m * 64:m * 64 + 64],
                                   i=bank[:, 0:64]: v.tensor_copy(o, i))
                bank_evac[m % 2] = (sACT if m % 2 == 0 else sDVE, y_evac[m])
            y_ready = (y_evac[CC - 2], y_evac[CC - 1])
            ctxf_readers = sPE.n

            wait("vector", sACT, y_ready[0])
            wait("vector", sDVE, y_ready[1])
            wait("scalar", sACT, y_ready[0])
            wait("scalar", sDVE, y_ready[1])
            wait("scalar", sPE, lnsc_readers)
            ln2_done = emit_ln(yb[:, :], 1, lnsc[:, :], "ts")
            wait("gpsimd", sDVE, ln2_done)
            add1 = op("gpsimd", lambda p, a=xb[:, :], b=yb[:, :]:
                      p.tensor_tensor(a, a, b, ALU.add))
            yb_free = ("gpsimd", add1)

            wait("sync", sPOOL, add1)
            xmid_store_done[g] = dma("sync", xmid_d[R:R + SUB, :], xb[:, :], s_xmst)

            wait("scalar", sPOOL, add1)
            wait("vector", sPOOL, add1)
            ln3_done = emit_ln(xb[:, :], 2, lnsc[:, :], "act", lnsc[:, :])
            xb_last_act[g % 2] = ln3_done

            wait("tensor", sACT, ln3_done)

            def _w_ln3(b, pbf, ss=ss):
                dst = ln3f[par][:, :].rearrange("p (j t) -> p j t", t=S)[
                    :, b * 8:(b + 1) * 8, ss * 128:(ss + 1) * 128]
                srcv = pbf.rearrange("p (j t) -> p j t", t=128)
                return op("scalar", lambda s, o=dst, i=srcv: s.copy(o, i))
            t3_pe, _ = emit_transpose16(lnsc, _w_ln3,
                                        pre_vec_wait=(sPE, ln3_readers[par]))
            lnsc_readers = t3_pe
            while ticks:
                drain_tick()

        def emit_ffn_macro(sup, mac):
            par = sup % 2
            ln3_rhs = ln3f[par]
            ticks = []
            prev_down_mark = gated_free
            nstep_base = (sup * NMAC + mac) * NSTEP

            def pe_gu():
                for st_ in range(NSTEP):
                    s = mac * NSTEP + st_
                    gstep = nstep_base + st_
                    p = s % 2
                    wait("sync", s_cT, conv_target(mac))
                    wait("sync", sPE, wgt_free[p])
                    gv = scr_gT.rearrange("(j q) H -> q j H", q=128)
                    uv = scr_uT.rearrange("(j q) H -> q j H", q=128)
                    for q4 in range(4):
                        dma("sync",
                            wgT[p][:, q4 * 512:(q4 + 1) * 512].rearrange("p (j h) -> p j h", h=128),
                            gv[:, q4 * 4:(q4 + 1) * 4, s * 128:(s + 1) * 128], s_w[p])
                    for q4 in range(4):
                        dma("sync",
                            wuT[p][:, q4 * 512:(q4 + 1) * 512].rearrange("p (j h) -> p j h", h=128),
                            uv[:, q4 * 4:(q4 + 1) * 4, s * 128:(s + 1) * 128], s_w[p])
                    wait("tensor", s_w[p], s_w[p].n)
                    if mac == 0 and st_ == 0:
                        wait("tensor", sACT, ln3_ready[par])
                    wait("tensor", s_silu, gstep)
                    gate_done = 0
                    for j in range(16):
                        gate_done = op("tensor", lambda t, o=psum[0][:, :],
                                       l=wgT[p][:, j * 128:(j + 1) * 128],
                                       r=ln3_rhs[:, j * S:(j + 1) * S],
                                       st=(j == 0), sp=(j == 15):
                                       t.matmul(o, l, r, start=st, stop=sp))
                    wait("tensor", s_gated, gstep)
                    up_done = 0
                    for j in range(16):
                        up_done = op("tensor", lambda t, o=psum[1][:, :],
                                     l=wuT[p][:, j * 128:(j + 1) * 128],
                                     r=ln3_rhs[:, j * S:(j + 1) * S],
                                     st=(j == 0), sp=(j == 15):
                                     t.matmul(o, l, r, start=st, stop=sp))
                    wgt_free[p] = sPE.n

                    def tick_step(st_=st_, gstep=gstep, gate_done=gate_done,
                                  up_done=up_done):
                        wait("scalar", sPE, gate_done)
                        wait("scalar", s_gated, gstep)
                        th = op("scalar", lambda sc: sc.activation(
                            silu_sb[:, :], psum[0][:, :], AF.Tanh, scale=0.5))
                        op_det("scalar", lambda sc: sc.copy(
                            gate_sb[:, :], psum[0][:, :]), s_silu)
                        wait("vector", sACT, th)
                        op("vector", lambda v: v.scalar_tensor_tensor(
                            silu_sb[:, :], silu_sb[:, :], 1.0, gate_sb[:, :],
                            ALU.add, ALU.mult))
                        wait("vector", sPE, up_done)
                        if st_ == 0:
                            wait("vector", sPE, prev_down_mark)
                        op_det("vector", lambda v, o=gated[:, st_ * S:(st_ + 1) * S]:
                               v.scalar_tensor_tensor(o, silu_sb[:, :], 0.5,
                                                      psum[1][:, :], ALU.mult, ALU.mult),
                               s_gated)
                    ticks.append(tick_step)

            def pe_down():
                nonlocal gated_free
                for d4 in range(4):
                    dn = [0, 0, 0, 0]
                    for mc in range(NSTEP):
                        p = mc % 2
                        wait("sync", sPE, wdt_free[p])
                        dv = scr_dT.rearrange("(c q) D -> q c D", q=128)
                        dma("sync", wdT[p][:, :],
                            dv[:, mac * NSTEP + mc, d4 * 512:(d4 + 1) * 512], s_wd[p])
                        wait("tensor", s_wd[p], s_wd[p].n)
                        if mc == 0:
                            wait("tensor", s_gated, nstep_base + NSTEP)
                            wait("tensor", s_dnev, (sup * NMAC + mac) * 16 + d4 * 4)
                        for tt in range(4):
                            dn[tt] = op("tensor", lambda t, o=psum[2 + tt][:, :],
                                        l=gated[:, mc * S + tt * 128:mc * S + (tt + 1) * 128],
                                        r=wdT[p][:, :],
                                        st=(mc == 0), sp=(mc == NSTEP - 1):
                                        t.matmul(o, l, r, start=st, stop=sp))
                        wdt_free[p] = sPE.n

                    def mk_evac(d4=d4, dn=tuple(dn)):
                        def tick_evac(tt, mark):
                            wait("vector", sPE, mark)
                            aslice = acc[:, tt * DM + d4 * 512: tt * DM + (d4 + 1) * 512]
                            if mac == 0:
                                op_det("vector", lambda v, o=aslice, i=psum[2 + tt][:, :]:
                                       v.tensor_copy(o, i), s_dnev)
                            else:
                                op_det("vector", lambda v, o=aslice, i=psum[2 + tt][:, :]:
                                       v.tensor_tensor(o, o, i, ALU.add), s_dnev)
                        for tt in range(4):
                            ticks.append(lambda tt=tt, dn=dn, f=tick_evac: f(tt, dn[tt]))
                    mk_evac()
                gated_free = sPE.n

            return pe_gu, pe_down, ticks

        def emit_ffn_tail(sup):
            nonlocal xm_free
            par = sup % 2
            ln3_readers[par] = sPE.n
            wait("vector", s_dnev, (sup + 1) * NMAC * 16)
            wait("scalar", s_dnev, (sup + 1) * NMAC * 16)
            wait("scalar", sPE, lnsc_readers)
            c = lambda t, i: stt[:, t * 8 + i:t * 8 + i + 1]
            asl = [acc[:, tt * DM:(tt + 1) * DM] for tt in range(NSUB)]
            for tt in range(NSUB):
                op("vector", lambda v, o=c(tt, 0), x=asl[tt]: v.reduce_sum(o, x, AX.X))
            sq_last = 0
            for tt in range(NSUB):
                sq_last = op("scalar", lambda s, o=lnsc[:, :], x=asl[tt], a=c(tt, 1):
                             s.activation(o, x, AF.Square, accum_out=a))
            wait("vector", sACT, sq_last)
            m2_last = 0
            for tt in range(NSUB):
                m2_last = op("vector", lambda v, o=c(tt, 2), i0=c(tt, 0):
                             v.tensor_scalar(o, i0, i0, inv_n2, ALU.mult, ALU.mult))
            wait("vector", sDVE, m2_last)
            ve_last = 0
            for tt in range(NSUB):
                ve_last = op("vector", lambda v, o=c(tt, 3), i0=c(tt, 1), i1=c(tt, 2):
                             v.scalar_tensor_tensor(o, i0, inv_n, i1, ALU.mult, ALU.subtract))
            wait("scalar", sDVE, ve_last)
            for tt in range(NSUB):
                op("scalar", lambda s, o=c(tt, 6), i=c(tt, 3):
                   s.activation(o, i, AF.Ln, bias=epsT[:, :]))
            r_last = 0
            for tt in range(NSUB):
                r_last = op("scalar", lambda s, o=c(tt, 4), i=c(tt, 6):
                            s.activation(o, i, AF.Exp, scale=-0.5))
            wait("vector", sACT, r_last)
            b_last = 0
            for tt in range(NSUB):
                b_last = op("vector", lambda v, o=c(tt, 5), i0=c(tt, 0), r=c(tt, 4):
                            v.tensor_scalar(o, i0, r, -inv_n, ALU.mult, ALU.mult))
            wait("vector", sDVE, b_last)
            for tt in range(NSUB):
                op("vector", lambda v, x=asl[tt], r=c(tt, 4), b=c(tt, 5):
                   v.tensor_scalar(x, x, r, b, ALU.mult, ALU.add))
            for tt in range(NSUB):
                g = sup * NSUB + tt
                R = g * SUB
                wait("sync", s_xmst, xmid_store_done[g])
                wait("sync", sDVE, xm_free)
                wait("sync", sACT, max(xb_last_act))
                wait("sync", s_out, s_out.n)
                xml = dma("sync", xm[:, :], xmid_d[R:R + SUB, :], s_x)
                wait("vector", s_x, xml)
                addf = op("vector", lambda v, a=xm[:, :], b=asl[tt]:
                          v.tensor_tensor(a, a, b, ALU.add))
                xm_free = addf
                wait("sync", sDVE, addf)
                dma("sync", out_t[R:R + SUB, :], xm[:, :], s_out)

        for _grp in range(4):
            emit_conv_group(_grp)
        x_load_done[0] = dma("sync", xbuf[0][:, :], x_in[0:SUB, :], s_x)
        x_load_done[1] = dma("sync", xbuf[1][:, :], x_in[SUB:2 * SUB, :], s_x)
        x_load_done[2] = dma("sync", xm[:, :], x_in[2 * SUB:3 * SUB, :], s_x)
        for _g in range(NSUB):
            emit_attn_subtile(_g)
        ln3_ready[0] = sACT.n
        convT_total_mark[0] = s_cT.n
        for sup in range(NSUP):
            for k in range(NMAC):
                pe_gu, pe_down, ticks = emit_ffn_macro(sup, k)
                if sup + 1 < NSUP:
                    emit_attn_subtile((sup + 1) * NSUB + k, pe_gu, pe_down, ticks)
                else:
                    pe_gu()
                    while ticks[:16]:
                        ticks.pop(0)()
                    pe_down()
                    while ticks:
                        ticks.pop(0)()
            if sup + 1 < NSUP:
                ln3_ready[(sup + 1) % 2] = sACT.n
            emit_ffn_tail(sup)
        if debug:
            wait("sync", sDVE, sDVE.n)
            wait("sync", sACT, sACT.n)
            wait("sync", sPOOL, sPOOL.n)
            wait("sync", sPE, sPE.n)
            for nm, sbuf in (("qkv", qkv), ("ctx", ctx), ("ctxf", ctxf), ("yb", yb),
                             ("xmid_last", xbuf[1]), ("ln1f", ln1f), ("lnsc", lnsc),
                             ("Zt", Zt), ("Wt", Wt), ("P0", P0),
                             ("P1", P1[0]), ("acc", acc), ("stt", stt), ("gated", gated),
                             ("wgT", wgT[1])):
                dma("sync", dbg[nm], sbuf[:, :], s_out)
            dma("sync", dbg["ln3f"], ln3f[1][:, :], s_out)
            dma("sync", dbg["xmid_all"], xmid_d, s_out)
        wait("sync", s_out, s_out.n)
        wait("sync", s_xmst, s_xmst.n)
        wait("sync", s_conv, s_conv.n)

        with nc.allow_low_precision(reason="bf16 grid/ctx intermediates"), \
                nc.Block() as block:
            @block.sync
            def _(e):
                for f in sched["sync"]:
                    f(e)

            @block.gpsimd
            def _(e):
                for f in sched["gpsimd"]:
                    f(e)

            @block.tensor
            def _(e):
                for f in sched["tensor"]:
                    f(e)

            @block.vector
            def _(e):
                for f in sched["vector"]:
                    f(e)

            @block.scalar
            def _(e):
                for f in sched["scalar"]:
                    f(e)

    return nc


_NC_CACHE = [None]


def kernel(L0_attn_x, L0_attn_Wqkv, L0_attn_bqkv, L0_attn_Wout, L0_attn_bout,
           L0_ffn_w_gate, L0_ffn_w_up, L0_ffn_w_down):
    x = np.ascontiguousarray(np.asarray(L0_attn_x, dtype=np.float32)).reshape(16384, 2048)
    wqkv = np.ascontiguousarray(np.asarray(L0_attn_Wqkv, dtype=np.float32))
    bqkv = np.ascontiguousarray(np.asarray(L0_attn_bqkv, dtype=np.float32))
    wout = np.ascontiguousarray(np.asarray(L0_attn_Wout, dtype=np.float32))
    bout = np.ascontiguousarray(np.asarray(L0_attn_bout, dtype=np.float32))
    wg = np.ascontiguousarray(np.asarray(L0_ffn_w_gate, dtype=np.float32))
    wu = np.ascontiguousarray(np.asarray(L0_ffn_w_up, dtype=np.float32))
    wd = np.ascontiguousarray(np.asarray(L0_ffn_w_down, dtype=np.float32))

    if _NC_CACHE[0] is None:
        _NC_CACHE[0] = build_program()
    nc = _NC_CACHE[0]

    in_maps = []
    for i in range(NCORES):
        in_maps.append({
            "x": x[i * TOK:(i + 1) * TOK],
            "wqkv": wqkv, "bqkv": bqkv, "wout": wout, "bout": bout,
            "wg": wg, "wu": wu, "wd": wd,
        })
    res = run_bass_kernel_spmd(nc, in_maps, core_ids=list(range(NCORES)))
    out = np.concatenate([np.asarray(r["out"]) for r in res.results], axis=0)
    return out.reshape(4, 4096, 2048)


# version 19
# speedup vs baseline: 1.0052x; 1.0047x over previous
import contextlib
import numpy as np
import concourse.bass as bass
import concourse.mybir as mybir
from concourse.bass_utils import run_bass_kernel_spmd

F32 = mybir.dt.float32
BF16 = mybir.dt.bfloat16
AF = mybir.ActivationFunctionType
ALU = mybir.AluOpType
AX = mybir.AxisListType

NCORES = 8
TOK = 2048
DM = 2048
CC, KH = 32, 64
HID = 8192
S = 512
NSUP = TOK // S
SUB = 128
NSUB = S // SUB
DG = 4
NDG = KH // DG
MACRO = 2048
NMAC = HID // MACRO
NSTEP = MACRO // 128
EPS = 1e-5
POOL_GRID_FRAC = 2


def build_program(debug=False):
    nc = bass.Bass("TRN2", target_bir_lowering=False)

    x_in = nc.dram_tensor("x", [TOK, DM], F32, kind="ExternalInput").ap()
    wqkv = nc.dram_tensor("wqkv", [CC, KH, 3 * KH], F32, kind="ExternalInput").ap()
    bqkv = nc.dram_tensor("bqkv", [CC, 3 * KH], F32, kind="ExternalInput").ap()
    wout = nc.dram_tensor("wout", [KH, KH], F32, kind="ExternalInput").ap()
    bout = nc.dram_tensor("bout", [KH], F32, kind="ExternalInput").ap()
    wg = nc.dram_tensor("wg", [HID, DM], F32, kind="ExternalInput").ap()
    wu = nc.dram_tensor("wu", [HID, DM], F32, kind="ExternalInput").ap()
    wd = nc.dram_tensor("wd", [DM, HID], F32, kind="ExternalInput").ap()
    out_t = nc.dram_tensor("out", [TOK, DM], F32, kind="ExternalOutput").ap()

    dbg = {}
    if debug:
        def dbg_t(name, shape, dt):
            dbg[name] = nc.dram_tensor("dbg_" + name, shape, dt, kind="ExternalOutput").ap()
        dbg_t("qkv", [128, CC * 192], BF16)
        dbg_t("ctx", [128, DM], BF16)
        dbg_t("ctxf", [128, DM], BF16)
        dbg_t("yb", [128, DM], F32)
        dbg_t("xmid_last", [128, DM], F32)
        dbg_t("ln1f", [128, DM], BF16)
        dbg_t("lnsc", [128, DM], BF16)
        dbg_t("Zt", [128, 32 * DG], F32)
        dbg_t("Wt", [128, 32 * DG], BF16)
        dbg_t("P0", [128, 2048], BF16)
        dbg_t("P1", [128, 2048], BF16)
        dbg_t("acc", [128, NSUB * DM], F32)
        dbg_t("ln3f", [128, 16 * S], BF16)
        dbg_t("stt", [128, 32], F32)
        dbg_t("gated", [128, NSTEP * S], BF16)
        dbg_t("xmid_all", [TOK, DM], F32)
        dbg_t("wgT", [128, 2048], BF16)

    wg_bf = nc.dram_tensor("wg_bf", [HID, DM], BF16, kind="Internal").ap()
    wu_bf = nc.dram_tensor("wu_bf", [HID, DM], BF16, kind="Internal").ap()
    wd_bf = nc.dram_tensor("wd_bf", [DM, HID], BF16, kind="Internal").ap()
    scr_gT = nc.dram_tensor("scr_gT", [DM, HID], BF16, kind="Internal").ap()
    scr_uT = nc.dram_tensor("scr_uT", [DM, HID], BF16, kind="Internal").ap()
    scr_dT = nc.dram_tensor("scr_dT", [HID, DM], BF16, kind="Internal").ap()
    xmid_d = nc.dram_tensor("xmid", [TOK, DM], F32, kind="Internal").ap()

    stk = contextlib.ExitStack()
    with stk:
        def sb(name, shape, dt):
            return stk.enter_context(nc.sbuf_tensor(name, shape, dt))

        acc = sb("acc", [128, NSUB * DM], F32)
        ln3f = [sb("ln3a", [128, 16 * S], BF16), sb("ln3b", [128, 16 * S], BF16)]
        wgT = [sb("wgT0", [128, 2048], BF16), sb("wgT1", [128, 2048], BF16)]
        wuT = [sb("wuT0", [128, 2048], BF16), sb("wuT1", [128, 2048], BF16)]
        wdT = [sb("wdT0", [128, 512], BF16), sb("wdT1", [128, 512], BF16)]
        gated = sb("gated", [128, NSTEP * S], BF16)
        silu_sb = sb("silu", [128, 512], BF16)
        gate_sb = sb("gate_sb", [128, 512], BF16)
        qkv = sb("qkv", [128, CC * 192], BF16)
        P0 = sb("P0", [128, 1024 * DG], BF16)
        P1 = [sb("P1a", [128, 1024 * DG], BF16), sb("P1b", [128, 1024 * DG], BF16)]
        Zt = sb("Zt", [128, 32 * DG], F32)
        Wt = sb("Wt", [128, 32 * DG], BF16)
        ctx = sb("ctx", [128, DM], BF16)
        ctxf = sb("ctxf", [128, DM], BF16)
        yb = sb("yb", [128, DM], BF16)
        lnsc = sb("lnsc", [128, DM], BF16)
        ln1f = sb("ln1f", [128, DM], BF16)
        xbuf = [sb("xa", [128, DM], F32), sb("xb", [128, DM], F32)]
        xm = sb("xm", [128, DM], F32)
        wq_s = sb("wq", [128, 16 * 192], BF16)
        wo_s = sb("wo", [128, 64], BF16)
        bq_s = sb("bq", [1, CC * 192], BF16)
        bo_s = sb("bo", [1, 64], BF16)
        ones = sb("ones", [1, 128], BF16)
        ident = sb("ident", [128, 128], BF16)
        stt = sb("stats", [128, 32], F32)
        epsT = sb("epsT", [128, 1], F32)
        mnegT = sb("mnegT", [128, 1], F32)
        zsc = sb("zsc", [128, 2048], BF16)

        import os as _os
        if _os.environ.get("SBUF_DEBUG"):
            print("SBUF remaining after allocs:", nc.sbuf_bytes_remaining)
        psum = [stk.enter_context(nc.psum_tensor(f"ps{i}", [128, 512], F32))
                for i in range(8)]

        class Sem:
            __slots__ = ("h", "n")

            def __init__(self, name):
                self.h = stk.enter_context(nc.semaphore(name))
                self.n = 0

        s_x = Sem("s_x")
        s_xmst = Sem("s_xmst")
        s_out = Sem("s_out")
        s_attw = Sem("s_attw")
        s_conv = Sem("s_conv")
        s_w = [Sem("s_w0"), Sem("s_w1")]
        s_cT = Sem("s_cT")
        s_silu = Sem("s_silu")
        s_gated = Sem("s_gated")
        s_dnev = Sem("s_dnev")
        s_wd = [Sem("s_wd0"), Sem("s_wd1")]
        sPE = Sem("sPE")
        sDVE = Sem("sDVE")
        sACT = Sem("sACT")
        sPOOL = Sem("sPOOL")
        ESEM = {"tensor": sPE, "vector": sDVE, "scalar": sACT, "gpsimd": sPOOL}

        sched = {k: [] for k in ("sync", "gpsimd", "tensor", "vector", "scalar")}

        def em(eng, fn):
            sched[eng].append(fn)

        def wait(eng, sem, val=None):
            v = sem.n if val is None else val
            if v > 0:
                em(eng, lambda e, h=sem.h, vv=v: e.wait_ge(h, vv))

        def op(eng, fn):
            se = ESEM[eng]
            se.n += 1
            em(eng, lambda e, f=fn, h=se.h: f(e).then_inc(h, 1))
            return se.n

        def op_det(eng, fn, sem):
            sem.n += 1
            em(eng, lambda e, f=fn, h=sem.h: f(e).then_inc(h, 1))
            return sem.n

        def dma(eng, dst, src, sem, transpose=False):
            sem.n += 16
            em(eng, lambda e, d=dst, s=src, h=sem.h, t=transpose:
               e.dma_start(d, s, transpose=t).then_inc(h, 16))
            return sem.n

        mz = op("gpsimd", lambda g: g.memset(ident[:, :], 0.0))
        wait("gpsimd", sPOOL, mz)
        op("gpsimd", lambda g: g.affine_select(
            out=ident[:, :], in_=ident[:, :], compare_op=ALU.not_equal,
            fill=1.0, base=0, pattern=[[-1, 128]], channel_multiplier=1))
        op("gpsimd", lambda g: g.memset(ones[:, :], 1.0))
        op("gpsimd", lambda g: g.memset(epsT[:, :], EPS))
        pool_consts = op("gpsimd", lambda g: g.memset(mnegT[:, :], -2.0))

        wq_re = (wqkv.rearrange("(cp ch) k i -> cp ch k i", ch=2)
                 .transpose([1, 2, 0, 3])
                 .rearrange("ch k cp i -> (ch k) cp i"))
        wq_dst = wq_s[:, :].rearrange("p (cp i) -> p cp i", i=192)
        for j in range(4):
            dma("gpsimd", wq_dst[:, j * 4:(j + 1) * 4, :],
                wq_re[:, j * 4:(j + 1) * 4, :], s_attw)
        dma("gpsimd", wo_s[0:64, :], wout, s_attw)
        dma("gpsimd", wo_s[64:128, :], wout, s_attw)
        dma("gpsimd", bq_s[0:1, :], bqkv.rearrange("c i -> (c i)").unsqueeze(0), s_attw)
        dma("gpsimd", bo_s[0:1, :], bout.unsqueeze(0), s_attw)
        attw_done = s_attw.n
        prologue_hooks = []

        conv_groups = []
        for grp in range(4):
            rows = slice(grp * 2048, (grp + 1) * 2048)
            grp_dmas = [(wg[rows, :], wg_bf[rows, :], False),
                        (wu[rows, :], wu_bf[rows, :], False)]
            for q in range(4):
                r2 = slice(q * 512, (q + 1) * 512)
                grp_dmas.append((wd[r2, grp * 2048:(grp + 1) * 2048],
                                 wd_bf[r2, grp * 2048:(grp + 1) * 2048], False))
            conv_groups.append(grp_dmas)
        conv_emitted = [0]

        def emit_conv_group(grp):
            if conv_emitted[0] > grp:
                return
            conv_emitted[0] = grp + 1
            for s_ap, d_ap, _ in conv_groups[grp]:
                dma("gpsimd", d_ap, s_ap, s_conv)

        def conv_cast_target(grp):
            return 16 * 6 * (grp + 1)

        convT_units = []
        for grp in range(4):
            for j in range(16):
                convT_units.append((wg_bf[grp * 2048:(grp + 1) * 2048, j * 128:(j + 1) * 128],
                                    scr_gT[j * 128:(j + 1) * 128, grp * 2048:(grp + 1) * 2048], grp))
            for j in range(16):
                convT_units.append((wu_bf[grp * 2048:(grp + 1) * 2048, j * 128:(j + 1) * 128],
                                    scr_uT[j * 128:(j + 1) * 128, grp * 2048:(grp + 1) * 2048], grp))
            for hc in range(16):
                h0 = grp * 2048 + hc * 128
                convT_units.append((wd_bf[0:2048, h0:h0 + 128],
                                    scr_dT[h0:h0 + 128, :], grp))
        convT_next = [0]
        convT_store_done = {}
        convT_group_mark = {}

        def emit_convT(nunits, convT_bufs):
            for _ in range(nunits):
                u = convT_next[0]
                if u >= len(convT_units):
                    return
                srcb, dstb, grp = convT_units[u]
                buf = convT_bufs[u % 2]
                wait("sync", s_conv, conv_cast_target(grp))
                if u >= 2:
                    wait("sync", s_cT, convT_store_done[u - 2])
                t = dma("sync", buf, srcb, s_cT, transpose=True)
                wait("sync", s_cT, t)
                convT_store_done[u] = dma("sync", dstb, buf, s_cT)
                convT_group_mark[grp] = convT_store_done[u]
                convT_next[0] = u + 1

        def conv_target(grp):
            return convT_group_mark[grp]

        inv_n = 1.0 / DM
        inv_n2 = inv_n * inv_n

        def emit_ln(xap, set_i, sq_out, apply_kind, apply_out=None):
            base = set_i * 8
            c = lambda i: stt[:, base + i:base + i + 1]
            su_done = op("vector", lambda v, o=c(0), x=xap: v.reduce_sum(o, x, AX.X))
            sq = op("scalar", lambda s, o=sq_out, x=xap, a=c(1):
                    s.activation(o, x, AF.Square, accum_out=a))
            wait("vector", sACT, sq)
            wait("vector", sDVE, su_done)
            m2 = op("vector", lambda v, o=c(2), i0=c(0):
                    v.tensor_scalar(o, i0, i0, inv_n2, ALU.mult, ALU.mult))
            wait("vector", sDVE, m2)
            ve = op("vector", lambda v, o=c(3), i0=c(1), i1=c(2):
                    v.scalar_tensor_tensor(o, i0, inv_n, i1, ALU.mult, ALU.subtract))
            wait("scalar", sDVE, ve)
            op("scalar", lambda s, o=c(6), i=c(3): s.activation(o, i, AF.Ln, bias=epsT[:, :]))
            r_done = op("scalar", lambda s, o=c(4), i=c(6):
                        s.activation(o, i, AF.Exp, scale=-0.5))
            wait("vector", sACT, r_done)
            b_done = op("vector", lambda v, o=c(5), i0=c(0), r=c(4):
                        v.tensor_scalar(o, i0, r, -inv_n, ALU.mult, ALU.mult))
            if apply_kind == "act":
                wait("scalar", sDVE, b_done)
                wait("scalar", sACT, r_done)
                done = op("scalar", lambda s, o=apply_out, x=xap, r=c(4), b=c(5):
                          s.activation(o, x, AF.Identity, bias=b, scale=r))
            else:
                wait("vector", sDVE, b_done)
                done = op("vector", lambda v, x=xap, r=c(4), b=c(5):
                          v.tensor_scalar(x, x, r, b, ALU.mult, ALU.add))
            return done

        bank_evac = [(sACT, 0), (sACT, 0)]

        def emit_transpose16(src_sb, dst_fn, pre_vec_wait=None):
            last_pe = 0
            for b in range(2):
                bank = psum[6 + b]
                pbf = bank[:, :].bitcast(BF16)
                wait("tensor", bank_evac[b][0], bank_evac[b][1])
                wait("tensor", sPOOL, pool_consts)
                for jj in range(8):
                    j = b * 8 + jj
                    last_pe = op("tensor", lambda t, o=pbf[:, jj * 128:(jj + 1) * 128],
                                 i=src_sb[:, j * 128:(j + 1) * 128]:
                                 t.transpose(o, i, ident[:, :]))
                wait("scalar", sPE, last_pe)
                if pre_vec_wait is not None and b == 0:
                    wait("scalar", pre_vec_wait[0], pre_vec_wait[1])
                bank_evac[b] = (sACT, dst_fn(b, pbf))
            return last_pe, bank_evac[1][1]

        x_load_done = [0] * (TOK // SUB)
        xmid_store_done = [0] * (TOK // SUB)
        ln3_ready = [0, 0]
        ln3_readers = [0, 0]
        wgt_free = [0, 0]
        wdt_free = [0, 0]
        qkv_grid_readers = {"vector": 0, "gpsimd": 0}
        lnsc_readers = 0
        ln1f_readers = 0
        ctxf_readers = 0
        xb_last_act = [0, 0]
        yb_free = 0
        ctx_free = 0
        gated_free = 0
        grid_exp_prev = 0
        grid_ctx_prev = 0
        grid_ctx_par = [0, 0]
        xm_free = 0

        wait("scalar", sPOOL, pool_consts)
        convT_total_mark = [0]
        def emit_attn_subtile(g, pe_gu=None, pe_down=None, ticks=None):
            ticks = ticks if ticks is not None else []

            def drain_tick():
                if ticks:
                    ticks.pop(0)()
            nonlocal lnsc_readers, ln1f_readers, ctxf_readers, yb_free, ctx_free
            nonlocal grid_exp_prev, grid_ctx_prev, qkv_grid_readers, xm_free
            sup = g // NSUB
            par = sup % 2
            ss = g % NSUB
            if g == NSUB:
                wait("scalar", s_cT, convT_total_mark[0])
            R = g * SUB
            xb = xbuf[g % 2]
            E_S = "vector"
            E_M = "vector"
            if g == 2:
                xb = xm

            if g >= 3:
                wait("sync", s_xmst, xmid_store_done[g - 2])
                wait("sync", sACT, xb_last_act[g % 2])
                x_load_done[g] = dma("sync", xb[:, :], x_in[R:R + SUB, :], s_x)
            if g < NSUB:
                convT_bufs = [ln3f[1][:, 0:2048], ln3f[1][:, 2048:4096]]
                emit_convT(48, convT_bufs)

            wait("vector", s_x, x_load_done[g])
            wait("scalar", s_x, x_load_done[g])
            wait("scalar", sPE, lnsc_readers)
            ln1_done = emit_ln(xb[:, :], 0, lnsc[:, :], "act", lnsc[:, :])

            wait("tensor", sACT, ln1_done)

            def _w_ln1(b, pbf):
                return op("scalar", lambda s, o=ln1f[:, b * 1024:(b + 1) * 1024],
                          i=pbf: s.copy(o, i))
            t1_pe, t1_evac = emit_transpose16(lnsc, _w_ln1,
                                              pre_vec_wait=(sPE, ln1f_readers))
            lnsc_readers = t1_pe

            wait("tensor", s_attw, attw_done)
            wait("tensor", sACT, t1_evac)
            qkv_evac = {}
            for cc in range(CC):
                bank = psum[6 + (cc % 2)]
                if cc >= 2:
                    wait("tensor", sACT if cc % 2 == 0 else sDVE, qkv_evac[cc - 2])
                ph, pc = cc % 2, cc // 2
                op("tensor", lambda t, o=bank[:, 0:192],
                   l=ln1f[ph * 64:ph * 64 + 64, pc * 128:pc * 128 + 128],
                   r=wq_s[ph * 64:ph * 64 + 64, pc * 192:pc * 192 + 192]:
                   t.matmul(o, l, r, start=True, stop=False))
                mm = op("tensor", lambda t, o=bank[:, 0:192],
                        r=bq_s[0:1, cc * 192:cc * 192 + 192]:
                        t.matmul(o, ones[0:1, :], r, start=False, stop=True))
                eng = "scalar" if cc % 2 == 0 else "vector"
                wait(eng, sPE, mm)
                if cc < 2:
                    wait(eng, sDVE, qkv_grid_readers["vector"])
                    wait(eng, sPOOL, qkv_grid_readers["gpsimd"])
                if eng == "scalar":
                    qkv_evac[cc] = op(eng, lambda s, o=qkv[:, cc * 192:cc * 192 + 192],
                                      i=bank[:, 0:192]: s.copy(o, i))
                else:
                    qkv_evac[cc] = op(eng, lambda v, o=qkv[:, cc * 192:cc * 192 + 192],
                                      i=bank[:, 0:192]: v.tensor_copy(o, i))
                bank_evac[cc % 2] = (sACT if cc % 2 == 0 else sDVE, qkv_evac[cc])
            qkv_ready = (qkv_evac[CC - 2], qkv_evac[CC - 1])
            ln1f_readers = sPE.n


            if pe_gu is not None:
                pe_gu()

            qkv5 = qkv[:, :].rearrange("p (c s dh dr) -> p s dh c dr",
                                       s=3, dh=NDG, dr=DG)
            P0v = P0[:, :].rearrange("p (e c dr) -> p e c dr", c=CC, dr=DG)
            Ztv = Zt[:, :].rearrange("p (c dr) -> p c dr", dr=DG)
            Wtv = Wt[:, :].rearrange("p (c dr) -> p c dr", dr=DG)
            Wtb = Wtv.unsqueeze(1).broadcast_to([128, CC, CC, DG])
            ctx4 = ctx[:, :].rearrange("p (m dh dr) -> p m dh dr", dh=NDG, dr=DG)

            def qkb(sel, dg):
                if sel == 0:
                    return qkv5[:, 0, dg].unsqueeze(1).broadcast_to([128, CC, CC, DG])
                return qkv5[:, 1, dg].unsqueeze(2).broadcast_to([128, CC, CC, DG])

            wait(E_S, sACT, max(grid_exp_prev, qkv_ready[0]))
            wait(E_S, sDVE, qkv_ready[1])
            s_mark = op(E_S, lambda e, o=P0v, a=qkb(0, 0), b=qkb(1, 0):
                        e.tensor_tensor(o, a, b, ALU.mult))
            for dg in range(NDG):
                pp = dg % 2
                P1t = P1[pp]
                P1v = P1t[:, :].rearrange("p (e c dr) -> p e c dr", c=CC, dr=DG)
                P1z = P1t[:, :].rearrange("p (e c dr) -> p c dr e", c=CC, dr=DG)
                P1m = P1t[:, :].rearrange("p (e c dr) -> p e dr c", c=CC, dr=DG)
                vv = qkv5[:, 2, dg]
                wait("scalar", ESEM[E_S], s_mark)
                wait("scalar", sDVE, grid_ctx_par[pp])
                exp_mark = op("scalar", lambda s, o=P1t[:, :], i=P0[:, :]:
                              s.activation(o, i, AF.Exp, scale=0.125,
                                           bias=mnegT[:, :]))
                grid_exp_prev = exp_mark
                if dg + 1 < NDG:
                    wait(E_S, sACT, exp_mark)
                    s_mark = op(E_S, lambda e, o=P0v, a=qkb(0, dg + 1), b=qkb(1, dg + 1):
                                e.tensor_tensor(o, a, b, ALU.mult))
                wait("vector", sACT, exp_mark)
                op("vector", lambda v, o=zsc[:, 0:2048], a=P1t[:, 0:2048],
                   b=P1t[:, 2048:4096]: v.tensor_tensor(o, a, b, ALU.add))
                op("vector", lambda v, o=zsc[:, 0:1024], a=zsc[:, 0:1024],
                   b=zsc[:, 1024:2048]: v.tensor_tensor(o, a, b, ALU.add))
                zv8 = zsc[:, 0:1024].rearrange("p (e c dr) -> p c dr e", c=CC, dr=DG)
                z_mark = op("vector", lambda v, o=Ztv, i=zv8:
                            v.tensor_reduce(o, i, AX.X, ALU.add))
                wait("vector", sDVE, z_mark)
                rz_done = op("vector", lambda v, o=Zt[:, :], i=Zt[:, :]:
                             v.reciprocal(o, i))
                wait("vector", sDVE, rz_done)
                wdone = op("vector", lambda v, o=Wtv, a=vv, b=Ztv:
                           v.tensor_tensor(o, a, b, ALU.mult))
                gdone = op("vector", lambda e, o=P1v, a=P1v, b=Wtb:
                           e.tensor_tensor(o, a, b, ALU.mult))
                if dg == 0:
                    wait("vector", sPE, ctx_free)
                grid_ctx_par[pp] = op("vector", lambda v, o=ctx4[:, :, dg, :], i=P1m:
                                      v.tensor_reduce(o, i, AX.X, ALU.add))
                grid_ctx_prev = grid_ctx_par[pp]
                if dg == 6 and pe_down is not None:
                    pe_down()
                ndr = 3 if dg < 6 else 4
                for _ in range(ndr):
                    drain_tick()
            qkv_grid_readers = {"vector": sDVE.n, "gpsimd": sPOOL.n}

            wait("tensor", sDVE, grid_ctx_prev)

            def _w_ctx(b, pbf):
                return op("scalar", lambda s, o=ctxf[:, b * 1024:(b + 1) * 1024],
                          i=pbf: s.copy(o, i))
            ct_pe, ct_evac = emit_transpose16(ctx, _w_ctx,
                                              pre_vec_wait=(sPE, ctxf_readers))
            ctx_free = ct_pe

            wait("tensor", sACT, ct_evac)
            y_evac = {}
            for m in range(CC):
                bank = psum[6 + (m % 2)]
                if m >= 2:
                    wait("tensor", sACT if m % 2 == 0 else sDVE, y_evac[m - 2])
                ph, pc = m % 2, m // 2
                op("tensor", lambda t, o=bank[:, 0:64],
                   l=ctxf[ph * 64:ph * 64 + 64, pc * 128:pc * 128 + 128],
                   r=wo_s[ph * 64:ph * 64 + 64, :]:
                   t.matmul(o, l, r, start=True, stop=False))
                mm = op("tensor", lambda t, o=bank[:, 0:64]:
                        t.matmul(o, ones[0:1, :], bo_s[0:1, :], start=False, stop=True))
                eng = "scalar" if m % 2 == 0 else "vector"
                wait(eng, sPE, mm)
                if m < 2:
                    wait(eng, sDVE, yb_free)
                if eng == "scalar":
                    y_evac[m] = op(eng, lambda s, o=yb[:, m * 64:m * 64 + 64],
                                   i=bank[:, 0:64]: s.copy(o, i))
                else:
                    y_evac[m] = op(eng, lambda v, o=yb[:, m * 64:m * 64 + 64],
                                   i=bank[:, 0:64]: v.tensor_copy(o, i))
                bank_evac[m % 2] = (sACT if m % 2 == 0 else sDVE, y_evac[m])
            y_ready = (y_evac[CC - 2], y_evac[CC - 1])
            ctxf_readers = sPE.n

            wait("vector", sACT, y_ready[0])
            wait("vector", sDVE, y_ready[1])
            wait("scalar", sACT, y_ready[0])
            wait("scalar", sDVE, y_ready[1])
            wait("scalar", sPE, lnsc_readers)
            ln2_done = emit_ln(yb[:, :], 1, lnsc[:, :], "ts")
            add1 = op("vector", lambda v, a=xb[:, :], b=yb[:, :]:
                      v.tensor_tensor(a, a, b, ALU.add))
            yb_free = add1

            wait("sync", sDVE, add1)
            xmid_store_done[g] = dma("sync", xmid_d[R:R + SUB, :], xb[:, :], s_xmst)

            wait("scalar", sDVE, add1)
            ln3_done = emit_ln(xb[:, :], 2, lnsc[:, :], "act", lnsc[:, :])
            xb_last_act[g % 2] = ln3_done

            wait("tensor", sACT, ln3_done)

            def _w_ln3(b, pbf, ss=ss):
                dst = ln3f[par][:, :].rearrange("p (j t) -> p j t", t=S)[
                    :, b * 8:(b + 1) * 8, ss * 128:(ss + 1) * 128]
                srcv = pbf.rearrange("p (j t) -> p j t", t=128)
                return op("scalar", lambda s, o=dst, i=srcv: s.copy(o, i))
            t3_pe, _ = emit_transpose16(lnsc, _w_ln3,
                                        pre_vec_wait=(sPE, ln3_readers[par]))
            lnsc_readers = t3_pe
            while ticks:
                drain_tick()

        def emit_ffn_macro(sup, mac):
            par = sup % 2
            ln3_rhs = ln3f[par]
            ticks = []
            prev_down_mark = gated_free
            nstep_base = (sup * NMAC + mac) * NSTEP

            def pe_gu():
                for st_ in range(NSTEP):
                    s = mac * NSTEP + st_
                    gstep = nstep_base + st_
                    p = s % 2
                    wait("sync", s_cT, conv_target(mac))
                    wait("sync", sPE, wgt_free[p])
                    gv = scr_gT.rearrange("(j q) H -> q j H", q=128)
                    uv = scr_uT.rearrange("(j q) H -> q j H", q=128)
                    for q4 in range(4):
                        dma("sync",
                            wgT[p][:, q4 * 512:(q4 + 1) * 512].rearrange("p (j h) -> p j h", h=128),
                            gv[:, q4 * 4:(q4 + 1) * 4, s * 128:(s + 1) * 128], s_w[p])
                    for q4 in range(4):
                        dma("sync",
                            wuT[p][:, q4 * 512:(q4 + 1) * 512].rearrange("p (j h) -> p j h", h=128),
                            uv[:, q4 * 4:(q4 + 1) * 4, s * 128:(s + 1) * 128], s_w[p])
                    wait("tensor", s_w[p], s_w[p].n)
                    if mac == 0 and st_ == 0:
                        wait("tensor", sACT, ln3_ready[par])
                    wait("tensor", s_silu, gstep)
                    gate_done = 0
                    for j in range(16):
                        gate_done = op("tensor", lambda t, o=psum[0][:, :],
                                       l=wgT[p][:, j * 128:(j + 1) * 128],
                                       r=ln3_rhs[:, j * S:(j + 1) * S],
                                       st=(j == 0), sp=(j == 15):
                                       t.matmul(o, l, r, start=st, stop=sp))
                    wait("tensor", s_gated, gstep)
                    up_done = 0
                    for j in range(16):
                        up_done = op("tensor", lambda t, o=psum[1][:, :],
                                     l=wuT[p][:, j * 128:(j + 1) * 128],
                                     r=ln3_rhs[:, j * S:(j + 1) * S],
                                     st=(j == 0), sp=(j == 15):
                                     t.matmul(o, l, r, start=st, stop=sp))
                    wgt_free[p] = sPE.n

                    def tick_step(st_=st_, gstep=gstep, gate_done=gate_done,
                                  up_done=up_done):
                        wait("scalar", sPE, gate_done)
                        wait("scalar", s_gated, gstep)
                        th = op("scalar", lambda sc: sc.activation(
                            silu_sb[:, :], psum[0][:, :], AF.Tanh, scale=0.5))
                        op_det("scalar", lambda sc: sc.copy(
                            gate_sb[:, :], psum[0][:, :]), s_silu)
                        wait("vector", sACT, th)
                        op("vector", lambda v: v.scalar_tensor_tensor(
                            silu_sb[:, :], silu_sb[:, :], 1.0, gate_sb[:, :],
                            ALU.add, ALU.mult))
                        wait("vector", sPE, up_done)
                        if st_ == 0:
                            wait("vector", sPE, prev_down_mark)
                        op_det("vector", lambda v, o=gated[:, st_ * S:(st_ + 1) * S]:
                               v.scalar_tensor_tensor(o, silu_sb[:, :], 0.5,
                                                      psum[1][:, :], ALU.mult, ALU.mult),
                               s_gated)
                    ticks.append(tick_step)

            def pe_down():
                nonlocal gated_free
                for d4 in range(4):
                    dn = [0, 0, 0, 0]
                    for mc in range(NSTEP):
                        p = mc % 2
                        wait("sync", sPE, wdt_free[p])
                        dv = scr_dT.rearrange("(c q) D -> q c D", q=128)
                        dma("sync", wdT[p][:, :],
                            dv[:, mac * NSTEP + mc, d4 * 512:(d4 + 1) * 512], s_wd[p])
                        wait("tensor", s_wd[p], s_wd[p].n)
                        if mc == 0:
                            wait("tensor", s_gated, nstep_base + NSTEP)
                            wait("tensor", s_dnev, (sup * NMAC + mac) * 16 + d4 * 4)
                        for tt in range(4):
                            dn[tt] = op("tensor", lambda t, o=psum[2 + tt][:, :],
                                        l=gated[:, mc * S + tt * 128:mc * S + (tt + 1) * 128],
                                        r=wdT[p][:, :],
                                        st=(mc == 0), sp=(mc == NSTEP - 1):
                                        t.matmul(o, l, r, start=st, stop=sp))
                        wdt_free[p] = sPE.n

                    def mk_evac(d4=d4, dn=tuple(dn)):
                        def tick_evac(tt, mark):
                            wait("vector", sPE, mark)
                            aslice = acc[:, tt * DM + d4 * 512: tt * DM + (d4 + 1) * 512]
                            if mac == 0:
                                op_det("vector", lambda v, o=aslice, i=psum[2 + tt][:, :]:
                                       v.tensor_copy(o, i), s_dnev)
                            else:
                                op_det("vector", lambda v, o=aslice, i=psum[2 + tt][:, :]:
                                       v.tensor_tensor(o, o, i, ALU.add), s_dnev)
                        for tt in range(4):
                            ticks.append(lambda tt=tt, dn=dn, f=tick_evac: f(tt, dn[tt]))
                    mk_evac()
                gated_free = sPE.n

            return pe_gu, pe_down, ticks

        def emit_ffn_tail(sup):
            nonlocal xm_free
            par = sup % 2
            ln3_readers[par] = sPE.n
            wait("vector", s_dnev, (sup + 1) * NMAC * 16)
            wait("scalar", s_dnev, (sup + 1) * NMAC * 16)
            wait("scalar", sPE, lnsc_readers)
            c = lambda t, i: stt[:, t * 8 + i:t * 8 + i + 1]
            asl = [acc[:, tt * DM:(tt + 1) * DM] for tt in range(NSUB)]
            for tt in range(NSUB):
                op("vector", lambda v, o=c(tt, 0), x=asl[tt]: v.reduce_sum(o, x, AX.X))
            sq_last = 0
            for tt in range(NSUB):
                sq_last = op("scalar", lambda s, o=lnsc[:, :], x=asl[tt], a=c(tt, 1):
                             s.activation(o, x, AF.Square, accum_out=a))
            wait("vector", sACT, sq_last)
            m2_last = 0
            for tt in range(NSUB):
                m2_last = op("vector", lambda v, o=c(tt, 2), i0=c(tt, 0):
                             v.tensor_scalar(o, i0, i0, inv_n2, ALU.mult, ALU.mult))
            wait("vector", sDVE, m2_last)
            ve_last = 0
            for tt in range(NSUB):
                ve_last = op("vector", lambda v, o=c(tt, 3), i0=c(tt, 1), i1=c(tt, 2):
                             v.scalar_tensor_tensor(o, i0, inv_n, i1, ALU.mult, ALU.subtract))
            wait("scalar", sDVE, ve_last)
            for tt in range(NSUB):
                op("scalar", lambda s, o=c(tt, 6), i=c(tt, 3):
                   s.activation(o, i, AF.Ln, bias=epsT[:, :]))
            r_last = 0
            for tt in range(NSUB):
                r_last = op("scalar", lambda s, o=c(tt, 4), i=c(tt, 6):
                            s.activation(o, i, AF.Exp, scale=-0.5))
            wait("vector", sACT, r_last)
            b_last = 0
            for tt in range(NSUB):
                b_last = op("vector", lambda v, o=c(tt, 5), i0=c(tt, 0), r=c(tt, 4):
                            v.tensor_scalar(o, i0, r, -inv_n, ALU.mult, ALU.mult))
            wait("vector", sDVE, b_last)
            for tt in range(NSUB):
                op("vector", lambda v, x=asl[tt], r=c(tt, 4), b=c(tt, 5):
                   v.tensor_scalar(x, x, r, b, ALU.mult, ALU.add))
            for tt in range(NSUB):
                g = sup * NSUB + tt
                R = g * SUB
                wait("sync", s_xmst, xmid_store_done[g])
                wait("sync", sDVE, xm_free)
                wait("sync", sACT, max(xb_last_act))
                wait("sync", s_out, s_out.n)
                xml = dma("sync", xm[:, :], xmid_d[R:R + SUB, :], s_x)
                wait("vector", s_x, xml)
                addf = op("vector", lambda v, a=xm[:, :], b=asl[tt]:
                          v.tensor_tensor(a, a, b, ALU.add))
                xm_free = addf
                wait("sync", sDVE, addf)
                dma("sync", out_t[R:R + SUB, :], xm[:, :], s_out)

        for _grp in range(4):
            emit_conv_group(_grp)
        x_load_done[0] = dma("sync", xbuf[0][:, :], x_in[0:SUB, :], s_x)
        x_load_done[1] = dma("sync", xbuf[1][:, :], x_in[SUB:2 * SUB, :], s_x)
        x_load_done[2] = dma("sync", xm[:, :], x_in[2 * SUB:3 * SUB, :], s_x)
        for _g in range(NSUB):
            emit_attn_subtile(_g)
        ln3_ready[0] = sACT.n
        convT_total_mark[0] = s_cT.n
        for sup in range(NSUP):
            for k in range(NMAC):
                pe_gu, pe_down, ticks = emit_ffn_macro(sup, k)
                if sup + 1 < NSUP:
                    emit_attn_subtile((sup + 1) * NSUB + k, pe_gu, pe_down, ticks)
                else:
                    pe_gu()
                    while ticks[:16]:
                        ticks.pop(0)()
                    pe_down()
                    while ticks:
                        ticks.pop(0)()
            if sup + 1 < NSUP:
                ln3_ready[(sup + 1) % 2] = sACT.n
            emit_ffn_tail(sup)
        if debug:
            wait("sync", sDVE, sDVE.n)
            wait("sync", sACT, sACT.n)
            wait("sync", sPOOL, sPOOL.n)
            wait("sync", sPE, sPE.n)
            for nm, sbuf in (("qkv", qkv), ("ctx", ctx), ("ctxf", ctxf), ("yb", yb),
                             ("xmid_last", xbuf[1]), ("ln1f", ln1f), ("lnsc", lnsc),
                             ("Zt", Zt), ("Wt", Wt), ("P0", P0),
                             ("P1", P1[0]), ("acc", acc), ("stt", stt), ("gated", gated),
                             ("wgT", wgT[1])):
                dma("sync", dbg[nm], sbuf[:, :], s_out)
            dma("sync", dbg["ln3f"], ln3f[1][:, :], s_out)
            dma("sync", dbg["xmid_all"], xmid_d, s_out)
        wait("sync", s_out, s_out.n)
        wait("sync", s_xmst, s_xmst.n)
        wait("sync", s_conv, s_conv.n)

        with nc.allow_low_precision(reason="bf16 grid/ctx intermediates"), \
                nc.Block() as block:
            @block.sync
            def _(e):
                for f in sched["sync"]:
                    f(e)

            @block.gpsimd
            def _(e):
                for f in sched["gpsimd"]:
                    f(e)

            @block.tensor
            def _(e):
                for f in sched["tensor"]:
                    f(e)

            @block.vector
            def _(e):
                for f in sched["vector"]:
                    f(e)

            @block.scalar
            def _(e):
                for f in sched["scalar"]:
                    f(e)

    return nc


_NC_CACHE = [None]


def kernel(L0_attn_x, L0_attn_Wqkv, L0_attn_bqkv, L0_attn_Wout, L0_attn_bout,
           L0_ffn_w_gate, L0_ffn_w_up, L0_ffn_w_down):
    x = np.ascontiguousarray(np.asarray(L0_attn_x, dtype=np.float32)).reshape(16384, 2048)
    wqkv = np.ascontiguousarray(np.asarray(L0_attn_Wqkv, dtype=np.float32))
    bqkv = np.ascontiguousarray(np.asarray(L0_attn_bqkv, dtype=np.float32))
    wout = np.ascontiguousarray(np.asarray(L0_attn_Wout, dtype=np.float32))
    bout = np.ascontiguousarray(np.asarray(L0_attn_bout, dtype=np.float32))
    wg = np.ascontiguousarray(np.asarray(L0_ffn_w_gate, dtype=np.float32))
    wu = np.ascontiguousarray(np.asarray(L0_ffn_w_up, dtype=np.float32))
    wd = np.ascontiguousarray(np.asarray(L0_ffn_w_down, dtype=np.float32))

    if _NC_CACHE[0] is None:
        _NC_CACHE[0] = build_program()
    nc = _NC_CACHE[0]

    in_maps = []
    for i in range(NCORES):
        in_maps.append({
            "x": x[i * TOK:(i + 1) * TOK],
            "wqkv": wqkv, "bqkv": bqkv, "wout": wout, "bout": bout,
            "wg": wg, "wu": wu, "wd": wd,
        })
    res = run_bass_kernel_spmd(nc, in_maps, core_ids=list(range(NCORES)))
    out = np.concatenate([np.asarray(r["out"]) for r in res.results], axis=0)
    return out.reshape(4, 4096, 2048)


# version 23
# speedup vs baseline: 1.0407x; 1.0354x over previous
import contextlib
import os
import numpy as np
import concourse.bass as bass
import concourse.mybir as mybir
from concourse.bass_utils import run_bass_kernel_spmd

F32 = mybir.dt.float32
BF16 = mybir.dt.bfloat16
AF = mybir.ActivationFunctionType
ALU = mybir.AluOpType
AX = mybir.AxisListType

NCORES = 8
TOK = 2048
DM = 2048
CC, KH = 32, 64
HID = 8192
S = 512
NSUP = TOK // S
SUB = 128
NSUB = S // SUB
DG = 4
NDG = KH // DG
MACRO = 2048
NMAC = HID // MACRO
NSTEP = MACRO // 128
EPS = 1e-5
POOL_GRID_FRAC = 2


def build_program(debug=False):
    nc = bass.Bass("TRN2", target_bir_lowering=False)

    x_in = nc.dram_tensor("x", [TOK, DM], F32, kind="ExternalInput").ap()
    wqkv = nc.dram_tensor("wqkv", [CC, KH, 3 * KH], F32, kind="ExternalInput").ap()
    bqkv = nc.dram_tensor("bqkv", [CC, 3 * KH], F32, kind="ExternalInput").ap()
    wout = nc.dram_tensor("wout", [KH, KH], F32, kind="ExternalInput").ap()
    bout = nc.dram_tensor("bout", [KH], F32, kind="ExternalInput").ap()
    wg = nc.dram_tensor("wg", [HID, DM], F32, kind="ExternalInput").ap()
    wu = nc.dram_tensor("wu", [HID, DM], F32, kind="ExternalInput").ap()
    wd = nc.dram_tensor("wd", [DM, HID], F32, kind="ExternalInput").ap()
    out_t = nc.dram_tensor("out", [TOK, DM], F32, kind="ExternalOutput").ap()

    dbg = {}
    if debug:
        def dbg_t(name, shape, dt):
            dbg[name] = nc.dram_tensor("dbg_" + name, shape, dt, kind="ExternalOutput").ap()
        dbg_t("qkv", [128, CC * 192], BF16)
        dbg_t("ctx", [128, DM], BF16)
        dbg_t("ctxf", [128, DM], BF16)
        dbg_t("yb", [128, DM], F32)
        dbg_t("xmid_last", [128, DM], F32)
        dbg_t("ln1f", [128, DM], BF16)
        dbg_t("lnsc", [128, DM], BF16)
        dbg_t("Zt", [128, 32 * DG], F32)
        dbg_t("Wt", [128, 32 * DG], BF16)
        dbg_t("P0", [128, 2048], BF16)
        dbg_t("P1", [128, 2048], BF16)
        dbg_t("acc", [128, NSUB * DM], F32)
        dbg_t("ln3f", [128, 16 * S], BF16)
        dbg_t("stt", [128, 32], F32)
        dbg_t("gated", [128, NSTEP * S], BF16)
        dbg_t("xmid_all", [TOK, DM], F32)
        dbg_t("wgT", [128, 2048], BF16)

    wg_bf = nc.dram_tensor("wg_bf", [HID, DM], BF16, kind="Internal").ap()
    wu_bf = nc.dram_tensor("wu_bf", [HID, DM], BF16, kind="Internal").ap()
    wd_bf = nc.dram_tensor("wd_bf", [DM, HID], BF16, kind="Internal").ap()
    scr_gT = nc.dram_tensor("scr_gT", [DM, HID], BF16, kind="Internal").ap()
    scr_uT = nc.dram_tensor("scr_uT", [DM, HID], BF16, kind="Internal").ap()
    scr_dT = nc.dram_tensor("scr_dT", [HID, DM], BF16, kind="Internal").ap()
    xmid_d = nc.dram_tensor("xmid", [TOK, DM], F32, kind="Internal").ap()

    stk = contextlib.ExitStack()
    with stk:
        def sb(name, shape, dt):
            return stk.enter_context(nc.sbuf_tensor(name, shape, dt))

        acc = sb("acc", [128, NSUB * DM], F32)
        ln3f = [sb("ln3a", [128, 16 * S], BF16), sb("ln3b", [128, 16 * S], BF16)]
        wgT = [sb("wgT0", [128, 2048], BF16), sb("wgT1", [128, 2048], BF16)]
        wuT = [sb("wuT0", [128, 2048], BF16), sb("wuT1", [128, 2048], BF16)]
        wdT = [sb("wdT0", [128, 512], BF16), sb("wdT1", [128, 512], BF16)]
        gated = sb("gated", [128, NSTEP * S], BF16)
        silu_sb = sb("silu", [128, 512], BF16)
        gate_sb = sb("gate_sb", [128, 512], BF16)
        qkv = sb("qkv", [128, CC * 192], BF16)
        P0 = sb("P0", [128, 1024 * DG], BF16)
        P1 = [sb("P1a", [128, 1024 * DG], BF16), sb("P1b", [128, 1024 * DG], BF16)]
        Zt = sb("Zt", [128, 32 * DG], F32)
        Wt = sb("Wt", [128, 32 * DG], BF16)
        ctx = sb("ctx", [128, DM], BF16)
        ctxf = sb("ctxf", [128, DM], BF16)
        yb = sb("yb", [128, DM], BF16)
        lnsc = sb("lnsc", [128, DM], BF16)
        ln1f = sb("ln1f", [128, DM], BF16)
        xbuf = [sb("xa", [128, DM], F32), sb("xb", [128, DM], F32)]
        xm = sb("xm", [128, DM], F32)
        wq_s = sb("wq", [128, 16 * 192], BF16)
        wo_s = sb("wo", [128, 64], BF16)
        bq_s = sb("bq", [1, CC * 192], BF16)
        bo_s = sb("bo", [1, 64], BF16)
        ones = sb("ones", [1, 128], BF16)
        ident = sb("ident", [128, 128], BF16)
        stt = sb("stats", [128, 32], F32)
        epsT = sb("epsT", [128, 1], F32)
        mnegT = sb("mnegT", [128, 1], F32)
        zsc = sb("zsc", [128, 2048], BF16)

        import os as _os
        if _os.environ.get("SBUF_DEBUG"):
            print("SBUF remaining after allocs:", nc.sbuf_bytes_remaining)
        psum = [stk.enter_context(nc.psum_tensor(f"ps{i}", [128, 512], F32))
                for i in range(8)]

        class Sem:
            __slots__ = ("h", "n")

            def __init__(self, name):
                self.h = stk.enter_context(nc.semaphore(name))
                self.n = 0

        s_x = Sem("s_x")
        s_xmst = Sem("s_xmst")
        s_out = Sem("s_out")
        s_attw = Sem("s_attw")
        s_conv = Sem("s_conv")
        s_w = [Sem("s_w0"), Sem("s_w1")]
        s_cT = Sem("s_cT")
        s_silu = Sem("s_silu")
        s_gated = Sem("s_gated")
        s_dnev = Sem("s_dnev")
        s_wd = [Sem("s_wd0"), Sem("s_wd1")]
        sPE = Sem("sPE")
        sDVE = Sem("sDVE")
        sACT = Sem("sACT")
        sPOOL = Sem("sPOOL")
        ESEM = {"tensor": sPE, "vector": sDVE, "scalar": sACT, "gpsimd": sPOOL}

        sched = {k: [] for k in ("sync", "gpsimd", "tensor", "vector", "scalar")}

        def em(eng, fn):
            sched[eng].append(fn)

        def wait(eng, sem, val=None):
            v = sem.n if val is None else val
            if v > 0:
                em(eng, lambda e, h=sem.h, vv=v: e.wait_ge(h, vv))

        def op(eng, fn):
            se = ESEM[eng]
            se.n += 1
            em(eng, lambda e, f=fn, h=se.h: f(e).then_inc(h, 1))
            return se.n

        def op_det(eng, fn, sem):
            sem.n += 1
            em(eng, lambda e, f=fn, h=sem.h: f(e).then_inc(h, 1))
            return sem.n

        def dma(eng, dst, src, sem, transpose=False):
            sem.n += 16
            em(eng, lambda e, d=dst, s=src, h=sem.h, t=transpose:
               e.dma_start(d, s, transpose=t).then_inc(h, 16))
            return sem.n

        mz = op("gpsimd", lambda g: g.memset(ident[:, :], 0.0))
        wait("gpsimd", sPOOL, mz)
        op("gpsimd", lambda g: g.affine_select(
            out=ident[:, :], in_=ident[:, :], compare_op=ALU.not_equal,
            fill=1.0, base=0, pattern=[[-1, 128]], channel_multiplier=1))
        op("gpsimd", lambda g: g.memset(ones[:, :], 1.0))
        op("gpsimd", lambda g: g.memset(epsT[:, :], EPS))
        pool_consts = op("gpsimd", lambda g: g.memset(mnegT[:, :], -2.0))

        wq_re = (wqkv.rearrange("(cp ch) k i -> cp ch k i", ch=2)
                 .transpose([1, 2, 0, 3])
                 .rearrange("ch k cp i -> (ch k) cp i"))
        wq_dst = wq_s[:, :].rearrange("p (cp i) -> p cp i", i=192)
        for j in range(4):
            dma("gpsimd", wq_dst[:, j * 4:(j + 1) * 4, :],
                wq_re[:, j * 4:(j + 1) * 4, :], s_attw)
        dma("gpsimd", wo_s[0:64, :], wout, s_attw)
        dma("gpsimd", wo_s[64:128, :], wout, s_attw)
        dma("gpsimd", bq_s[0:1, :], bqkv.rearrange("c i -> (c i)").unsqueeze(0), s_attw)
        dma("gpsimd", bo_s[0:1, :], bout.unsqueeze(0), s_attw)
        attw_done = s_attw.n
        prologue_hooks = []

        conv_groups = []
        for grp in range(4):
            rows = slice(grp * 2048, (grp + 1) * 2048)
            grp_dmas = [(wg[rows, :], wg_bf[rows, :], False),
                        (wu[rows, :], wu_bf[rows, :], False)]
            for q in range(4):
                r2 = slice(q * 512, (q + 1) * 512)
                grp_dmas.append((wd[r2, grp * 2048:(grp + 1) * 2048],
                                 wd_bf[r2, grp * 2048:(grp + 1) * 2048], False))
            conv_groups.append(grp_dmas)
        conv_emitted = [0]

        def emit_conv_group(grp):
            if conv_emitted[0] > grp:
                return
            conv_emitted[0] = grp + 1
            for s_ap, d_ap, _ in conv_groups[grp]:
                dma("gpsimd", d_ap, s_ap, s_conv)

        def conv_cast_target(grp):
            return 16 * 6 * (grp + 1)

        convT_units = []
        for grp in range(4):
            for j in range(16):
                convT_units.append((wg_bf[grp * 2048:(grp + 1) * 2048, j * 128:(j + 1) * 128],
                                    scr_gT[j * 128:(j + 1) * 128, grp * 2048:(grp + 1) * 2048], grp))
            for j in range(16):
                convT_units.append((wu_bf[grp * 2048:(grp + 1) * 2048, j * 128:(j + 1) * 128],
                                    scr_uT[j * 128:(j + 1) * 128, grp * 2048:(grp + 1) * 2048], grp))
            for hc in range(16):
                h0 = grp * 2048 + hc * 128
                convT_units.append((wd_bf[0:2048, h0:h0 + 128],
                                    scr_dT[h0:h0 + 128, :], grp))
        convT_next = [0]
        convT_store_done = {}
        convT_group_mark = {}

        def emit_convT(nunits, convT_bufs):
            for _ in range(nunits):
                u = convT_next[0]
                if u >= len(convT_units):
                    return
                srcb, dstb, grp = convT_units[u]
                buf = convT_bufs[u % 2]
                wait("sync", s_conv, conv_cast_target(grp))
                if u >= 2:
                    wait("sync", s_cT, convT_store_done[u - 2])
                t = dma("sync", buf, srcb, s_cT, transpose=True)
                wait("sync", s_cT, t)
                convT_store_done[u] = dma("sync", dstb, buf, s_cT)
                convT_group_mark[grp] = convT_store_done[u]
                convT_next[0] = u + 1

        def conv_target(grp):
            return convT_group_mark[grp]

        inv_n = 1.0 / DM
        inv_n2 = inv_n * inv_n

        def emit_ln(xap, set_i, sq_out, apply_kind, apply_out=None):
            base = set_i * 8
            c = lambda i: stt[:, base + i:base + i + 1]
            su_done = op("vector", lambda v, o=c(0), x=xap: v.reduce_sum(o, x, AX.X))
            sq = op("scalar", lambda s, o=sq_out, x=xap, a=c(1):
                    s.activation(o, x, AF.Square, accum_out=a))
            wait("vector", sACT, sq)
            wait("vector", sDVE, su_done)
            m2 = op("vector", lambda v, o=c(2), i0=c(0):
                    v.tensor_scalar(o, i0, i0, inv_n2, ALU.mult, ALU.mult))
            wait("vector", sDVE, m2)
            ve = op("vector", lambda v, o=c(3), i0=c(1), i1=c(2):
                    v.scalar_tensor_tensor(o, i0, inv_n, i1, ALU.mult, ALU.subtract))
            wait("scalar", sDVE, ve)
            op("scalar", lambda s, o=c(6), i=c(3): s.activation(o, i, AF.Ln, bias=epsT[:, :]))
            r_done = op("scalar", lambda s, o=c(4), i=c(6):
                        s.activation(o, i, AF.Exp, scale=-0.5))
            wait("vector", sACT, r_done)
            b_done = op("vector", lambda v, o=c(5), i0=c(0), r=c(4):
                        v.tensor_scalar(o, i0, r, -inv_n, ALU.mult, ALU.mult))
            if apply_kind == "act":
                wait("scalar", sDVE, b_done)
                wait("scalar", sACT, r_done)
                done = op("scalar", lambda s, o=apply_out, x=xap, r=c(4), b=c(5):
                          s.activation(o, x, AF.Identity, bias=b, scale=r))
            else:
                wait("vector", sDVE, b_done)
                done = op("vector", lambda v, x=xap, r=c(4), b=c(5):
                          v.tensor_scalar(x, x, r, b, ALU.mult, ALU.add))
            return done

        bank_evac = [(sACT, 0), (sACT, 0)]

        def emit_transpose16(src_sb, dst_fn, pre_vec_wait=None):
            last_pe = 0
            for b in range(2):
                bank = psum[6 + b]
                pbf = bank[:, :].bitcast(BF16)
                wait("tensor", bank_evac[b][0], bank_evac[b][1])
                wait("tensor", sPOOL, pool_consts)
                for jj in range(8):
                    j = b * 8 + jj
                    last_pe = op("tensor", lambda t, o=pbf[:, jj * 128:(jj + 1) * 128],
                                 i=src_sb[:, j * 128:(j + 1) * 128]:
                                 t.transpose(o, i, ident[:, :]))
                wait("scalar", sPE, last_pe)
                if pre_vec_wait is not None and b == 0:
                    wait("scalar", pre_vec_wait[0], pre_vec_wait[1])
                bank_evac[b] = (sACT, dst_fn(b, pbf))
            return last_pe, bank_evac[1][1]

        x_load_done = [0] * (TOK // SUB)
        xmid_store_done = [0] * (TOK // SUB)
        ln3_ready = [0, 0]
        ln3_readers = [0, 0]
        wgt_free = [0, 0]
        wdt_free = [0, 0]
        qkv_grid_readers = {"vector": 0, "gpsimd": 0}
        lnsc_readers = 0
        ln1f_readers = 0
        ctxf_readers = 0
        xb_last_act = [0, 0]
        yb_free = 0
        ctx_free = 0
        gated_free = 0
        grid_exp_prev = 0
        grid_ctx_prev = 0
        grid_ctx_par = [0, 0]
        xm_free = 0

        wait("scalar", sPOOL, pool_consts)
        convT_total_mark = [0]
        def emit_attn_subtile(g, pe_gu=None, pe_down=None, ticks=None,
                              next_gu_tail=None):
            ticks = ticks if ticks is not None else []

            def drain_tick():
                if ticks:
                    ticks.pop(0)()
            nonlocal lnsc_readers, ln1f_readers, ctxf_readers, yb_free, ctx_free
            nonlocal grid_exp_prev, grid_ctx_prev, qkv_grid_readers, xm_free
            sup = g // NSUB
            par = sup % 2
            ss = g % NSUB
            if g == NSUB:
                wait("scalar", s_cT, convT_total_mark[0])
            R = g * SUB
            xb = xbuf[g % 2]
            E_S = "vector"
            E_M = "vector"
            if g == 2:
                xb = xm

            if g >= 3:
                wait("sync", s_xmst, xmid_store_done[g - 2])
                wait("sync", sACT, xb_last_act[g % 2])
                x_load_done[g] = dma("sync", xb[:, :], x_in[R:R + SUB, :], s_x)
            if g < NSUB:
                convT_bufs = [ln3f[1][:, 0:2048], ln3f[1][:, 2048:4096]]
                emit_convT(48, convT_bufs)

            wait("vector", s_x, x_load_done[g])
            wait("scalar", s_x, x_load_done[g])
            wait("scalar", sPE, lnsc_readers)
            ln1_done = emit_ln(xb[:, :], 0, lnsc[:, :], "act", lnsc[:, :])

            wait("tensor", sACT, ln1_done)

            def _w_ln1(b, pbf):
                return op("scalar", lambda s, o=ln1f[:, b * 1024:(b + 1) * 1024],
                          i=pbf: s.copy(o, i))
            t1_pe, t1_evac = emit_transpose16(lnsc, _w_ln1,
                                              pre_vec_wait=(sPE, ln1f_readers))
            lnsc_readers = t1_pe

            wait("tensor", s_attw, attw_done)
            wait("tensor", sACT, t1_evac)
            qkv_evac = {}
            for cc in range(CC):
                bank = psum[6 + (cc % 2)]
                if cc >= 2:
                    wait("tensor", sACT if cc % 2 == 0 else sDVE, qkv_evac[cc - 2])
                ph, pc = cc % 2, cc // 2
                op("tensor", lambda t, o=bank[:, 0:192],
                   l=ln1f[ph * 64:ph * 64 + 64, pc * 128:pc * 128 + 128],
                   r=wq_s[ph * 64:ph * 64 + 64, pc * 192:pc * 192 + 192]:
                   t.matmul(o, l, r, start=True, stop=False))
                mm = op("tensor", lambda t, o=bank[:, 0:192],
                        r=bq_s[0:1, cc * 192:cc * 192 + 192]:
                        t.matmul(o, ones[0:1, :], r, start=False, stop=True))
                eng = "scalar" if cc % 2 == 0 else "vector"
                wait(eng, sPE, mm)
                if cc < 2:
                    wait(eng, sDVE, qkv_grid_readers["vector"])
                    wait(eng, sPOOL, qkv_grid_readers["gpsimd"])
                if eng == "scalar":
                    qkv_evac[cc] = op(eng, lambda s, o=qkv[:, cc * 192:cc * 192 + 192],
                                      i=bank[:, 0:192]: s.copy(o, i))
                else:
                    qkv_evac[cc] = op(eng, lambda v, o=qkv[:, cc * 192:cc * 192 + 192],
                                      i=bank[:, 0:192]: v.tensor_copy(o, i))
                bank_evac[cc % 2] = (sACT if cc % 2 == 0 else sDVE, qkv_evac[cc])
            qkv_ready = (qkv_evac[CC - 2], qkv_evac[CC - 1])
            ln1f_readers = sPE.n


            if pe_gu is not None:
                pe_gu()

            qkv5 = qkv[:, :].rearrange("p (c s dh dr) -> p s dh c dr",
                                       s=3, dh=NDG, dr=DG)
            P0v = P0[:, :].rearrange("p (e c dr) -> p e c dr", c=CC, dr=DG)
            Ztv = Zt[:, :].rearrange("p (c dr) -> p c dr", dr=DG)
            Wtv = Wt[:, :].rearrange("p (c dr) -> p c dr", dr=DG)
            Wtb = Wtv.unsqueeze(1).broadcast_to([128, CC, CC, DG])
            ctx4 = ctx[:, :].rearrange("p (m dh dr) -> p m dh dr", dh=NDG, dr=DG)

            def qkb(sel, dg):
                if sel == 0:
                    return qkv5[:, 0, dg].unsqueeze(1).broadcast_to([128, CC, CC, DG])
                return qkv5[:, 1, dg].unsqueeze(2).broadcast_to([128, CC, CC, DG])

            wait(E_S, sACT, max(grid_exp_prev, qkv_ready[0]))
            wait(E_S, sDVE, qkv_ready[1])
            s_mark = op(E_S, lambda e, o=P0v, a=qkb(0, 0), b=qkb(1, 0):
                        e.tensor_tensor(o, a, b, ALU.mult))
            for dg in range(NDG):
                pp = dg % 2
                P1t = P1[pp]
                P1v = P1t[:, :].rearrange("p (e c dr) -> p e c dr", c=CC, dr=DG)
                P1z = P1t[:, :].rearrange("p (e c dr) -> p c dr e", c=CC, dr=DG)
                P1m = P1t[:, :].rearrange("p (e c dr) -> p e dr c", c=CC, dr=DG)
                vv = qkv5[:, 2, dg]
                wait("scalar", ESEM[E_S], s_mark)
                wait("scalar", sDVE, grid_ctx_par[pp])
                exp_mark = op("scalar", lambda s, o=P1t[:, :], i=P0[:, :]:
                              s.activation(o, i, AF.Exp, scale=0.125,
                                           bias=mnegT[:, :]))
                grid_exp_prev = exp_mark
                if dg + 1 < NDG:
                    wait(E_S, sACT, exp_mark)
                    s_mark = op(E_S, lambda e, o=P0v, a=qkb(0, dg + 1), b=qkb(1, dg + 1):
                                e.tensor_tensor(o, a, b, ALU.mult))
                wait("vector", sACT, exp_mark)
                op("vector", lambda v, o=zsc[:, 0:2048], a=P1t[:, 0:2048],
                   b=P1t[:, 2048:4096]: v.tensor_tensor(o, a, b, ALU.add))
                op("vector", lambda v, o=zsc[:, 0:1024], a=zsc[:, 0:1024],
                   b=zsc[:, 1024:2048]: v.tensor_tensor(o, a, b, ALU.add))
                zv8 = zsc[:, 0:1024].rearrange("p (e c dr) -> p c dr e", c=CC, dr=DG)
                z_mark = op("vector", lambda v, o=Ztv, i=zv8:
                            v.tensor_reduce(o, i, AX.X, ALU.add))
                wait("vector", sDVE, z_mark)
                rz_done = op("vector", lambda v, o=Zt[:, :], i=Zt[:, :]:
                             v.reciprocal(o, i))
                wait("vector", sDVE, rz_done)
                wdone = op("vector", lambda v, o=Wtv, a=vv, b=Ztv:
                           v.tensor_tensor(o, a, b, ALU.mult))
                gdone = op("vector", lambda e, o=P1v, a=P1v, b=Wtb:
                           e.tensor_tensor(o, a, b, ALU.mult))
                if dg == 0:
                    wait("vector", sPE, ctx_free)
                grid_ctx_par[pp] = op("vector", lambda v, o=ctx4[:, :, dg, :], i=P1m:
                                      v.tensor_reduce(o, i, AX.X, ALU.add))
                grid_ctx_prev = grid_ctx_par[pp]
                if dg == 6 and pe_down is not None:
                    pe_down()
                drain_tick()
                drain_tick()
            qkv_grid_readers = {"vector": sDVE.n, "gpsimd": sPOOL.n}

            wait("tensor", sDVE, grid_ctx_prev)

            def _w_ctx(b, pbf):
                return op("scalar", lambda s, o=ctxf[:, b * 1024:(b + 1) * 1024],
                          i=pbf: s.copy(o, i))
            ct_pe, ct_evac = emit_transpose16(ctx, _w_ctx,
                                              pre_vec_wait=(sPE, ctxf_readers))
            ctx_free = ct_pe

            wait("tensor", sACT, ct_evac)
            y_evac = {}
            for m in range(CC):
                bank = psum[6 + (m % 2)]
                if m >= 2:
                    wait("tensor", sACT if m % 2 == 0 else sDVE, y_evac[m - 2])
                ph, pc = m % 2, m // 2
                op("tensor", lambda t, o=bank[:, 0:64],
                   l=ctxf[ph * 64:ph * 64 + 64, pc * 128:pc * 128 + 128],
                   r=wo_s[ph * 64:ph * 64 + 64, :]:
                   t.matmul(o, l, r, start=True, stop=False))
                mm = op("tensor", lambda t, o=bank[:, 0:64]:
                        t.matmul(o, ones[0:1, :], bo_s[0:1, :], start=False, stop=True))
                eng = "scalar" if m % 2 == 0 else "vector"
                wait(eng, sPE, mm)
                if m < 2:
                    wait(eng, sDVE, yb_free)
                if eng == "scalar":
                    y_evac[m] = op(eng, lambda s, o=yb[:, m * 64:m * 64 + 64],
                                   i=bank[:, 0:64]: s.copy(o, i))
                else:
                    y_evac[m] = op(eng, lambda v, o=yb[:, m * 64:m * 64 + 64],
                                   i=bank[:, 0:64]: v.tensor_copy(o, i))
                bank_evac[m % 2] = (sACT if m % 2 == 0 else sDVE, y_evac[m])
            y_ready = (y_evac[CC - 2], y_evac[CC - 1])
            ctxf_readers = sPE.n

            wait("vector", sACT, y_ready[0])
            wait("vector", sDVE, y_ready[1])
            wait("scalar", sACT, y_ready[0])
            wait("scalar", sDVE, y_ready[1])
            wait("scalar", sPE, lnsc_readers)
            ln2_done = emit_ln(yb[:, :], 1, lnsc[:, :], "ts")
            add1 = op("vector", lambda v, a=xb[:, :], b=yb[:, :]:
                      v.tensor_tensor(a, a, b, ALU.add))
            yb_free = add1

            wait("sync", sDVE, add1)
            xmid_store_done[g] = dma("sync", xmid_d[R:R + SUB, :], xb[:, :], s_xmst)

            wait("scalar", sDVE, add1)
            ln3_done = emit_ln(xb[:, :], 2, lnsc[:, :], "act", lnsc[:, :])
            xb_last_act[g % 2] = ln3_done

            wait("tensor", sACT, ln3_done)

            def _w_ln3(b, pbf, ss=ss):
                dst = ln3f[par][:, :].rearrange("p (j t) -> p j t", t=S)[
                    :, b * 8:(b + 1) * 8, ss * 128:(ss + 1) * 128]
                srcv = pbf.rearrange("p (j t) -> p j t", t=128)
                return op("scalar", lambda s, o=dst, i=srcv: s.copy(o, i))
            t3_pe, _ = emit_transpose16(lnsc, _w_ln3,
                                        pre_vec_wait=(sPE, ln3_readers[par]))
            lnsc_readers = t3_pe
            if next_gu_tail is not None:
                next_gu_tail()
            while ticks:
                drain_tick()

        def emit_ffn_macro(sup, mac):
            par = sup % 2
            ln3_rhs = ln3f[par]
            ticks = []
            nstep_base = (sup * NMAC + mac) * NSTEP

            def pe_gu_range(lo, hi):
                for st_ in range(lo, hi):
                    s = mac * NSTEP + st_
                    gstep = nstep_base + st_
                    p = s % 2
                    wait("sync", s_cT, conv_target(mac))
                    wait("sync", sPE, wgt_free[p])
                    gv = scr_gT.rearrange("(j q) H -> q j H", q=128)
                    uv = scr_uT.rearrange("(j q) H -> q j H", q=128)
                    for q4 in range(4):
                        dma("sync",
                            wgT[p][:, q4 * 512:(q4 + 1) * 512].rearrange("p (j h) -> p j h", h=128),
                            gv[:, q4 * 4:(q4 + 1) * 4, s * 128:(s + 1) * 128], s_w[p])
                    for q4 in range(4):
                        dma("sync",
                            wuT[p][:, q4 * 512:(q4 + 1) * 512].rearrange("p (j h) -> p j h", h=128),
                            uv[:, q4 * 4:(q4 + 1) * 4, s * 128:(s + 1) * 128], s_w[p])
                    wait("tensor", s_w[p], s_w[p].n)
                    if mac == 0 and st_ == 0:
                        wait("tensor", sACT, ln3_ready[par])
                    wait("tensor", s_silu, gstep)
                    gate_done = 0
                    for j in range(16):
                        gate_done = op("tensor", lambda t, o=psum[0][:, :],
                                       l=wgT[p][:, j * 128:(j + 1) * 128],
                                       r=ln3_rhs[:, j * S:(j + 1) * S],
                                       st=(j == 0), sp=(j == 15):
                                       t.matmul(o, l, r, start=st, stop=sp))
                    wait("tensor", s_gated, gstep)
                    up_done = 0
                    for j in range(16):
                        up_done = op("tensor", lambda t, o=psum[1][:, :],
                                     l=wuT[p][:, j * 128:(j + 1) * 128],
                                     r=ln3_rhs[:, j * S:(j + 1) * S],
                                     st=(j == 0), sp=(j == 15):
                                     t.matmul(o, l, r, start=st, stop=sp))
                    wgt_free[p] = sPE.n

                    def tick_step(st_=st_, gstep=gstep, gate_done=gate_done,
                                  up_done=up_done):
                        wait("scalar", sPE, gate_done)
                        wait("scalar", s_gated, gstep)
                        th = op("scalar", lambda sc: sc.activation(
                            silu_sb[:, :], psum[0][:, :], AF.Tanh, scale=0.5))
                        op_det("scalar", lambda sc: sc.copy(
                            gate_sb[:, :], psum[0][:, :]), s_silu)
                        wait("vector", sACT, th)
                        op("vector", lambda v: v.scalar_tensor_tensor(
                            silu_sb[:, :], silu_sb[:, :], 1.0, gate_sb[:, :],
                            ALU.add, ALU.mult))
                        wait("vector", sPE, up_done)
                        if st_ == 0:
                            wait("vector", sPE, gated_free)
                        op_det("vector", lambda v, o=gated[:, st_ * S:(st_ + 1) * S]:
                               v.scalar_tensor_tensor(o, silu_sb[:, :], 0.5,
                                                      psum[1][:, :], ALU.mult, ALU.mult),
                               s_gated)
                    ticks.append(tick_step)

            def pe_down():
                nonlocal gated_free
                for d4 in range(4):
                    dn = [0, 0, 0, 0]
                    for mc in range(NSTEP):
                        p = mc % 2
                        wait("sync", sPE, wdt_free[p])
                        dv = scr_dT.rearrange("(c q) D -> q c D", q=128)
                        dma("sync", wdT[p][:, :],
                            dv[:, mac * NSTEP + mc, d4 * 512:(d4 + 1) * 512], s_wd[p])
                        wait("tensor", s_wd[p], s_wd[p].n)
                        if mc == 0:
                            wait("tensor", s_gated, nstep_base + NSTEP)
                            wait("tensor", s_dnev, (sup * NMAC + mac) * 16 + d4 * 4)
                        for tt in range(4):
                            dn[tt] = op("tensor", lambda t, o=psum[2 + tt][:, :],
                                        l=gated[:, mc * S + tt * 128:mc * S + (tt + 1) * 128],
                                        r=wdT[p][:, :],
                                        st=(mc == 0), sp=(mc == NSTEP - 1):
                                        t.matmul(o, l, r, start=st, stop=sp))
                        wdt_free[p] = sPE.n

                    def mk_evac(d4=d4, dn=tuple(dn)):
                        def tick_evac(tt, mark):
                            wait("vector", sPE, mark)
                            aslice = acc[:, tt * DM + d4 * 512: tt * DM + (d4 + 1) * 512]
                            if mac == 0:
                                op_det("vector", lambda v, o=aslice, i=psum[2 + tt][:, :]:
                                       v.tensor_copy(o, i), s_dnev)
                            else:
                                op_det("vector", lambda v, o=aslice, i=psum[2 + tt][:, :]:
                                       v.tensor_tensor(o, o, i, ALU.add), s_dnev)
                        for tt in range(4):
                            ticks.append(lambda tt=tt, dn=dn, f=tick_evac: f(tt, dn[tt]))
                    mk_evac()
                gated_free = sPE.n

            GUSPLIT = int(os.environ.get("GUSPLIT", "0"))

            def pe_gu_tail():
                pe_gu_range(0, GUSPLIT)

            def pe_gu_head():
                pe_gu_range(GUSPLIT, NSTEP)

            return pe_gu_tail, pe_gu_head, pe_down, ticks

        def emit_ffn_tail(sup):
            nonlocal xm_free
            par = sup % 2
            ln3_readers[par] = sPE.n
            wait("vector", s_dnev, (sup + 1) * NMAC * 16)
            wait("scalar", s_dnev, (sup + 1) * NMAC * 16)
            wait("scalar", sPE, lnsc_readers)
            c = lambda t, i: stt[:, t * 8 + i:t * 8 + i + 1]
            asl = [acc[:, tt * DM:(tt + 1) * DM] for tt in range(NSUB)]
            for tt in range(NSUB):
                op("vector", lambda v, o=c(tt, 0), x=asl[tt]: v.reduce_sum(o, x, AX.X))
            sq_last = 0
            for tt in range(NSUB):
                sq_last = op("scalar", lambda s, o=lnsc[:, :], x=asl[tt], a=c(tt, 1):
                             s.activation(o, x, AF.Square, accum_out=a))
            wait("vector", sACT, sq_last)
            m2_last = 0
            for tt in range(NSUB):
                m2_last = op("vector", lambda v, o=c(tt, 2), i0=c(tt, 0):
                             v.tensor_scalar(o, i0, i0, inv_n2, ALU.mult, ALU.mult))
            wait("vector", sDVE, m2_last)
            ve_last = 0
            for tt in range(NSUB):
                ve_last = op("vector", lambda v, o=c(tt, 3), i0=c(tt, 1), i1=c(tt, 2):
                             v.scalar_tensor_tensor(o, i0, inv_n, i1, ALU.mult, ALU.subtract))
            wait("scalar", sDVE, ve_last)
            for tt in range(NSUB):
                op("scalar", lambda s, o=c(tt, 6), i=c(tt, 3):
                   s.activation(o, i, AF.Ln, bias=epsT[:, :]))
            r_last = 0
            for tt in range(NSUB):
                r_last = op("scalar", lambda s, o=c(tt, 4), i=c(tt, 6):
                            s.activation(o, i, AF.Exp, scale=-0.5))
            wait("vector", sACT, r_last)
            b_last = 0
            for tt in range(NSUB):
                b_last = op("vector", lambda v, o=c(tt, 5), i0=c(tt, 0), r=c(tt, 4):
                            v.tensor_scalar(o, i0, r, -inv_n, ALU.mult, ALU.mult))
            wait("vector", sDVE, b_last)
            for tt in range(NSUB):
                op("vector", lambda v, x=asl[tt], r=c(tt, 4), b=c(tt, 5):
                   v.tensor_scalar(x, x, r, b, ALU.mult, ALU.add))
            for tt in range(NSUB):
                g = sup * NSUB + tt
                R = g * SUB
                wait("sync", s_xmst, xmid_store_done[g])
                wait("sync", sDVE, xm_free)
                wait("sync", sACT, max(xb_last_act))
                wait("sync", s_out, s_out.n)
                xml = dma("sync", xm[:, :], xmid_d[R:R + SUB, :], s_x)
                wait("vector", s_x, xml)
                addf = op("vector", lambda v, a=xm[:, :], b=asl[tt]:
                          v.tensor_tensor(a, a, b, ALU.add))
                xm_free = addf
                wait("sync", sDVE, addf)
                dma("sync", out_t[R:R + SUB, :], xm[:, :], s_out)

        for _grp in range(4):
            emit_conv_group(_grp)
        x_load_done[0] = dma("sync", xbuf[0][:, :], x_in[0:SUB, :], s_x)
        x_load_done[1] = dma("sync", xbuf[1][:, :], x_in[SUB:2 * SUB, :], s_x)
        x_load_done[2] = dma("sync", xm[:, :], x_in[2 * SUB:3 * SUB, :], s_x)
        macro_fns = [None] * (NSUP * NMAC)

        def get_macro(i):
            if macro_fns[i] is None:
                macro_fns[i] = emit_ffn_macro(i // NMAC, i % NMAC)
            return macro_fns[i]

        def gu_tail_and_drain(i):
            tail, _, _, tks = get_macro(i)
            tail()
            for _ in range(min(len(tks), int(os.environ.get("GUSPLIT", "0")))):
                tks.pop(0)()

        for _g in range(NSUB - 1):
            emit_attn_subtile(_g)
        ln3_pre = sACT.n
        emit_attn_subtile(NSUB - 1)
        ln3_ready[0] = sACT.n
        convT_total_mark[0] = s_cT.n
        get_macro(0)[0]()
        for i in range(NSUP * NMAC):
            sup, k = i // NMAC, i % NMAC
            _, gu_head, pe_down, ticks = get_macro(i)
            g = NSUB + i
            if g < TOK // SUB:
                nxt = (lambda j=i + 1: gu_tail_and_drain(j)) if g % NSUB != NSUB - 1 else None
                emit_attn_subtile(g, gu_head, pe_down, ticks, next_gu_tail=nxt)
                if g % NSUB == NSUB - 1:
                    ln3_ready[(g // NSUB) % 2] = sACT.n
                    if i + 1 < NSUP * NMAC:
                        gu_tail_and_drain(i + 1)
            else:
                gu_head()
                while ticks[:16]:
                    ticks.pop(0)()
                pe_down()
                while ticks:
                    ticks.pop(0)()
                if i + 1 < NSUP * NMAC:
                    gu_tail_and_drain(i + 1)
            if k == NMAC - 1:
                emit_ffn_tail(sup)
        if debug:
            wait("sync", sDVE, sDVE.n)
            wait("sync", sACT, sACT.n)
            wait("sync", sPOOL, sPOOL.n)
            wait("sync", sPE, sPE.n)
            for nm, sbuf in (("qkv", qkv), ("ctx", ctx), ("ctxf", ctxf), ("yb", yb),
                             ("xmid_last", xbuf[1]), ("ln1f", ln1f), ("lnsc", lnsc),
                             ("Zt", Zt), ("Wt", Wt), ("P0", P0),
                             ("P1", P1[0]), ("acc", acc), ("stt", stt), ("gated", gated),
                             ("wgT", wgT[1])):
                dma("sync", dbg[nm], sbuf[:, :], s_out)
            dma("sync", dbg["ln3f"], ln3f[1][:, :], s_out)
            dma("sync", dbg["xmid_all"], xmid_d, s_out)
        wait("sync", s_out, s_out.n)
        wait("sync", s_xmst, s_xmst.n)
        wait("sync", s_conv, s_conv.n)

        with nc.allow_low_precision(reason="bf16 grid/ctx intermediates"), \
                nc.Block() as block:
            @block.sync
            def _(e):
                for f in sched["sync"]:
                    f(e)

            @block.gpsimd
            def _(e):
                for f in sched["gpsimd"]:
                    f(e)

            @block.tensor
            def _(e):
                for f in sched["tensor"]:
                    f(e)

            @block.vector
            def _(e):
                for f in sched["vector"]:
                    f(e)

            @block.scalar
            def _(e):
                for f in sched["scalar"]:
                    f(e)

    return nc


_NC_CACHE = [None]


def kernel(L0_attn_x, L0_attn_Wqkv, L0_attn_bqkv, L0_attn_Wout, L0_attn_bout,
           L0_ffn_w_gate, L0_ffn_w_up, L0_ffn_w_down):
    x = np.ascontiguousarray(np.asarray(L0_attn_x, dtype=np.float32)).reshape(16384, 2048)
    wqkv = np.ascontiguousarray(np.asarray(L0_attn_Wqkv, dtype=np.float32))
    bqkv = np.ascontiguousarray(np.asarray(L0_attn_bqkv, dtype=np.float32))
    wout = np.ascontiguousarray(np.asarray(L0_attn_Wout, dtype=np.float32))
    bout = np.ascontiguousarray(np.asarray(L0_attn_bout, dtype=np.float32))
    wg = np.ascontiguousarray(np.asarray(L0_ffn_w_gate, dtype=np.float32))
    wu = np.ascontiguousarray(np.asarray(L0_ffn_w_up, dtype=np.float32))
    wd = np.ascontiguousarray(np.asarray(L0_ffn_w_down, dtype=np.float32))

    if _NC_CACHE[0] is None:
        _NC_CACHE[0] = build_program()
    nc = _NC_CACHE[0]

    in_maps = []
    for i in range(NCORES):
        in_maps.append({
            "x": x[i * TOK:(i + 1) * TOK],
            "wqkv": wqkv, "bqkv": bqkv, "wout": wout, "bout": bout,
            "wg": wg, "wu": wu, "wd": wd,
        })
    res = run_bass_kernel_spmd(nc, in_maps, core_ids=list(range(NCORES)))
    out = np.concatenate([np.asarray(r["out"]) for r in res.results], axis=0)
    return out.reshape(4, 4096, 2048)
